# version 22
# baseline (speedup 1.0000x reference)
import numpy as np
import ml_dtypes
import concourse.bass as bass
import concourse.mybir as mybir
from concourse.bass_utils import run_bass_kernel_spmd
from contextlib import ExitStack

F32 = mybir.dt.float32
BF16 = mybir.dt.bfloat16
AF = mybir.ActivationFunctionType
ALU = mybir.AluOpType
AX = mybir.AxisListType

D = 2048
SEQ = 4096
OWN = 2048
NH = 16
DH = 128
FH = 8192
NDC = 16
NU = 64
SCALE_Q = 1.0 / float(np.sqrt(128.0))
S_SETS = ([0, 3, 4, 7], [1, 2, 5, 6])
KVS = [1024, 2048, 3072, 4096]
NEG = -1.0e30

ENGINES = ("tensor", "vector", "scalar", "gpsimd", "sync")


class Sem:
    __slots__ = ("h", "n", "name")

    def __init__(self, h, name):
        self.h = h
        self.n = 0
        self.name = name


class Rec:

    def __init__(self, nc, blk):
        self.nc = nc
        self.blk = blk
        self.ops = {e: [] for e in ENGINES}
        self.wm = {e: {} for e in ENGINES}

    def sem(self, name):
        return Sem(self.nc.alloc_semaphore(name), name)

    def op(self, eng, fn, waits=(), inc=None, inc_by=1):
        lst = self.ops[eng]
        for w in waits:
            if w is None:
                continue
            if isinstance(w, Sem):
                s, v = w, w.n
            else:
                s, v = w
            if v is None or v <= 0:
                continue
            if self.wm[eng].get(id(s), -1) < v:
                lst.append(("w", s, v))
                self.wm[eng][id(s)] = v
        if inc is not None:
            inc.n += inc_by
        if fn is not None:
            lst.append(("x", fn, inc, inc_by))
        return (inc, inc.n) if inc is not None else None

    def dma(self, q, out, in_, sem, waits=()):
        self.op(q, (lambda e, o=out, i=in_: e.dma_start(out=o, in_=i)),
                waits, inc=sem, inc_by=16)
        return (sem, sem.n)

    def dmaT(self, q, out, in_, sem, waits=()):
        self.op(q, (lambda e, o=out, i=in_: e.dma_start(out=o, in_=i,
                                                        transpose=True)),
                waits, inc=sem, inc_by=16)
        return (sem, sem.n)

    def fence(self, fences):
        for e in ENGINES:
            for (s, v) in fences:
                if v > 0 and self.wm[e].get(id(s), -1) < v:
                    self.ops[e].append(("w", s, v))
                    self.wm[e][id(s)] = v

    def flush(self):
        for e in ENGINES:
            lst = self.ops[e]
            if not lst:
                continue

            def body(eng, lst=tuple(lst)):
                for item in lst:
                    if item[0] == "w":
                        eng.wait_ge(item[1].h, item[2])
                    else:
                        _, fn, inc, k = item
                        ins = fn(eng)
                        if inc is not None:
                            ins.then_inc(inc.h, k)
            getattr(self.blk, e)(body)
            self.ops[e] = []


def _popcount_and(n):
    i = np.arange(n)
    pc = np.zeros((n, n), dtype=np.int64)
    a = i[:, None] & i[None, :]
    while np.any(a):
        pc += a & 1
        a >>= 1
    return pc


def _hadamard(n):
    return np.where(_popcount_and(n) % 2 == 0, 1.0, -1.0).astype(np.float32)


def _h64_i2():
    h64 = _hadamard(64)
    m = np.zeros((128, 128), dtype=np.float32)
    for r2 in range(2):
        m[r2::2, r2::2] = h64
    return m


def _amask(j):
    m = np.zeros((4, 4, 128, 1024), dtype=np.float32)
    for l in range(4):
        g = S_SETS[j][l]
        kvs = KVS[l]
        for i2 in range(4):
            qpos = g * 512 + i2 * 128 + np.arange(128)[:, None]
            jabs = kvs - 1024 + np.arange(1024)[None, :]
            m[l, i2] = np.where(jabs <= qpos, 0.0, NEG)
    return m.astype(ml_dtypes.bfloat16)


def _ln_tile(R, vS, aS, sX, sXsq, st, sXn, xld_wait, cEps, sq_guard=()):
    R.op("vector", lambda e, sX=sX, st=st: e.reduce_sum(st[:, 0:1], sX, axis=AX.X),
         waits=(xld_wait,), inc=vS)
    R.op("vector", lambda e, st=st: e.tensor_scalar_mul(st[:, 1:2], st[:, 0:1],
                                                        1.0 / D),
         waits=((vS, vS.n),), inc=vS)
    R.op("scalar", lambda e, sX=sX, st=st: e.activation(
        out=sXsq, in_=sX, func=AF.Square, accum_out=st[:, 2:3]),
        waits=(xld_wait,) + tuple(sq_guard), inc=aS)
    asq = (aS, aS.n)
    R.op("vector", lambda e, st=st: e.tensor_mul(st[:, 4:5], st[:, 1:2], st[:, 1:2]),
         waits=((vS, vS.n),), inc=vS)
    R.op("vector", lambda e, st=st: e.scalar_tensor_tensor(
        out=st[:, 3:4], in0=st[:, 2:3], scalar=1.0 / D, in1=st[:, 4:5],
        op0=ALU.mult, op1=ALU.subtract), waits=(asq, (vS, vS.n)), inc=vS)
    vvar = (vS, vS.n)
    R.op("scalar", lambda e, st=st, cEps=cEps: e.activation(
        out=st[:, 5:6], in_=st[:, 3:4], func=AF.Sqrt, bias=cEps[:, 0:1]),
        waits=(vvar,), inc=aS)
    asd = (aS, aS.n)
    R.op("vector", lambda e, st=st: e.reciprocal(st[:, 6:7], st[:, 5:6]),
         waits=(asd,), inc=vS)
    R.op("vector", lambda e, sX=sX, st=st, sXn=sXn: e.tensor_scalar(
        out=sXn, in0=sX, scalar1=st[:, 1:2], scalar2=st[:, 6:7],
        op0=ALU.subtract, op1=ALU.mult), waits=((vS, vS.n),), inc=vS)
    return (vS, vS.n)


def _xn_transposes(R, nc_t, sXn, sXnT, ts, pT, pT_free, tT, vS, cIdB,
                   xn_done, CONST, extra_guard, tkey):
    for k in range(4):
        pb = (tkey * 4 + k) % 2
        for q in range(4):
            dc = 4 * k + q
            w = (xn_done, CONST, pT_free[pb]) + tuple(extra_guard) if q == 0 else ()
            R.op("tensor",
                 lambda e, sXn=sXn, dc=dc, pb=pb, q=q: e.transpose(
                     pT[pb][:, 128 * q:128 * (q + 1)],
                     sXn[:, 128 * dc:128 * (dc + 1)], cIdB[:]),
                 waits=w, inc=tT if q == 3 else None)
        tdone = (tT, tT.n)
        R.op("vector",
             lambda e, k=k, ts=ts, pb=pb: e.tensor_copy(
                 out=sXnT[:, 4 * k:4 * (k + 1), 128 * ts:128 * (ts + 1)],
                 in_=pT[pb][:].rearrange("p (a b) -> p a b", a=4)),
             waits=(tdone,), inc=vS)
        pT_free[pb] = (vS, vS.n)


def build_program(debug=False):
    nc = bass.Bass("TRN2", target_bir_lowering=False, debug=False)

    xf = nc.dram_tensor("xf", [SEQ, D], F32, kind="ExternalInput").ap()
    xo = nc.dram_tensor("xo", [OWN, D], F32, kind="ExternalInput").ap()
    wq = nc.dram_tensor("wq", [D, D], F32, kind="ExternalInput").ap()
    wk = nc.dram_tensor("wk", [D, D], F32, kind="ExternalInput").ap()
    wv = nc.dram_tensor("wv", [D, D], F32, kind="ExternalInput").ap()
    wout = nc.dram_tensor("wout", [D, D], F32, kind="ExternalInput").ap()
    w1 = nc.dram_tensor("w1", [FH, D], F32, kind="ExternalInput").ap()
    w3 = nc.dram_tensor("w3", [FH, D], F32, kind="ExternalInput").ap()
    w2 = nc.dram_tensor("w2", [D, FH], F32, kind="ExternalInput").ap()
    amask = nc.dram_tensor("amask", [4, 4, 128, 1024], BF16, kind="ExternalInput").ap()
    h128 = nc.dram_tensor("h128", [128, 128], F32, kind="ExternalInput").ap()
    h64 = nc.dram_tensor("h64", [128, 128], F32, kind="ExternalInput").ap()
    identf = nc.dram_tensor("identf", [128, 128], F32, kind="ExternalInput").ap()
    out = nc.dram_tensor("out", [OWN, D], F32, kind="ExternalOutput").ap()

    WQB = nc.dram_tensor("WQB", [D, D], BF16, kind="Internal").ap()
    WKB = nc.dram_tensor("WKB", [D, D], BF16, kind="Internal").ap()
    WOB = nc.dram_tensor("WOB", [D, D], BF16, kind="Internal").ap()
    KS = nc.dram_tensor("KS", [D, SEQ], BF16, kind="Internal").ap()
    VS = nc.dram_tensor("VS", [NH, 128, 32, DH], BF16, kind="Internal").ap()
    QS = nc.dram_tensor("QS", [4, 128, NH, 512], BF16, kind="Internal").ap()
    XNT = nc.dram_tensor("XNT", [4, 128, NDC, 512], BF16, kind="Internal").ap()
    XFT = nc.dram_tensor("XFT", [8, 128, NDC, 512], BF16, kind="Internal").ap()
    AS = nc.dram_tensor("AS", [NH, 128, OWN], BF16, kind="Internal").ap()
    W13S = nc.dram_tensor("W13S", [NU, 128, 2 * D], BF16, kind="Internal").ap()
    W2S = nc.dram_tensor("W2S", [NU, 128, D], BF16, kind="Internal").ap()

    dbg = {}
    if debug:
        for name, t, shape in (("dKS", KS, [D, SEQ]), ("dVS", VS, [NH, 128, 32, DH]),
                               ("dQS", QS, [4, 128, NH, 512]),
                               ("dXNT", XNT, [4, 128, NDC, 512]),
                               ("dAS", AS, [NH, 128, OWN]),
                               ("dW13S", W13S, [NU, 128, 2 * D]),
                               ("dW2S", W2S, [NU, 128, D])):
            dbg[name] = (nc.dram_tensor(name, shape, BF16, kind="ExternalOutput").ap(), t)

    with ExitStack() as es_all:
        cIdB = es_all.enter_context(nc.sbuf_tensor("cIdB", [128, 128], BF16)).ap()
        cIdF = es_all.enter_context(nc.sbuf_tensor("cIdF", [128, 128], F32)).ap()
        cH128 = es_all.enter_context(nc.sbuf_tensor("cH128", [128, 128], BF16)).ap()
        cH64 = es_all.enter_context(nc.sbuf_tensor("cH64", [128, 128], BF16)).ap()
        cOnesF = es_all.enter_context(nc.sbuf_tensor("cOnesF", [1, 128], F32)).ap()
        cEps = es_all.enter_context(nc.sbuf_tensor("cEps", [128, 1], F32)).ap()

        blk = es_all.enter_context(nc.Block())
        R = Rec(nc, blk)

        cst = R.sem("cst")
        cstf = R.sem("cstf")
        R.dma("gpsimd", cIdB[:], identf[:], cst)
        R.dma("sync", cIdF[:], identf[:], cstf)
        R.dma("gpsimd", cH128[:], h128[:], cst)
        R.dma("gpsimd", cH64[:], h64[:], cst)
        CONST = (cst, cst.n)
        vcst = R.sem("vcst")
        R.op("vector", lambda e: e.memset(cOnesF[:], 1.0), inc=vcst)
        R.op("vector", lambda e: e.memset(cEps[:], 1.0e-5), inc=vcst)
        ONES = (vcst, vcst.n)

        WKR = R.dma("gpsimd", WKB[:, :], wk[:, :], R.sem("cwk"))
        WQR = R.dma("gpsimd", WQB[:, :], wq[:, :], R.sem("cwq"))
        WOR = R.dma("gpsimd", WOB[:, :], wout[:, :], R.sem("cwo"))

        H = dict(xf=xf, xo=xo, wq=WQB, wk=WKB, wv=wv, wout=WOB, w1=w1,
                 w3=w3, w2=w2, amask=amask, out=out, KS=KS, VS=VS, QS=QS,
                 XNT=XNT, XFT=XFT, AS=AS, W13S=W13S, W2S=W2S, cIdB=cIdB, cIdF=cIdF,
                 cH128=cH128, cH64=cH64, cOnesF=cOnesF, cEps=cEps, CONST=CONST,
                 ONES=ONES, WKR=WKR, WQR=WQR, WOR=WOR)

        R.fence([CONST, ONES, (cstf, cstf.n)])
        f1 = _phase12(nc, R, H)
        R.fence(f1)
        f3 = _phase3(nc, R, H)
        R.fence(f3)
        oSg = _phase4(nc, R, H)

        if debug:
            dsem = R.sem("dbgs")
            for name, (o, t) in dbg.items():
                R.dma("sync", o[:], t[:], dsem)
            R.op("sync", None, waits=((dsem, dsem.n),))

        R.op("sync", None, waits=tuple((s, s.n) for s in oSg))
        R.flush()

    return nc


def _phase12(nc, R, H):
    with ExitStack() as es:
        sX = es.enter_context(nc.sbuf_tensor("sX", [128, 2, D], F32)).ap()
        sStat = es.enter_context(nc.sbuf_tensor("sStat", [128, 2, 8], F32)).ap()
        sXn = es.enter_context(nc.sbuf_tensor("sXn", [128, 2, D], BF16)).ap()
        sXT = es.enter_context(nc.sbuf_tensor("sXT", [128, 2, NDC, 512], BF16)).ap()
        sW = es.enter_context(nc.sbuf_tensor("sW", [128, NDC, D], BF16)).ap()
        sVst = es.enter_context(nc.sbuf_tensor("sVst", [128, NH, 4, DH], BF16)).ap()
        sKev = es.enter_context(nc.sbuf_tensor("sKev", [128, 4, 512], BF16)).ap()
        sWld = es.enter_context(nc.sbuf_tensor("sWld", [128, 2, D], F32)).ap()
        sWldB = es.enter_context(nc.sbuf_tensor("sWldB", [128, 2, D], BF16)).ap()
        sWst = es.enter_context(nc.sbuf_tensor("sWst", [128, 2, D], BF16)).ap()
        sW2ldB = es.enter_context(nc.sbuf_tensor("sW2ldB", [128, 1, D], BF16)).ap()
        sStgA = es.enter_context(nc.sbuf_tensor("sStgA", [128, 16, 512], BF16)).ap()
        sW2ev = es.enter_context(nc.sbuf_tensor("sW2ev", [128, 4, 512], BF16)).ap()
        pT = [es.enter_context(nc.psum_tensor(f"pT{i}", [128, 512], BF16)).ap()
              for i in range(2)]
        pMM = [es.enter_context(nc.psum_tensor(f"pMM{i}", [128, 512], F32)).ap()
               for i in range(2)]
        pWT = [es.enter_context(nc.psum_tensor(f"pWT{i}", [128, 512], BF16)).ap()
               for i in range(2)]
        pSA = [es.enter_context(nc.psum_tensor(f"pSA{i}", [128, 512], F32)).ap()
               for i in range(2)]

        xldg = [R.sem("p1_xld0"), R.sem("p1_xld1")]
        xftg = [R.sem("p1_xft0"), R.sem("p1_xft1")]
        stg = [R.sem("p1_st0"), R.sem("p1_st1")]
        wres = R.sem("p1_wres")
        wldg = [R.sem("p1_wld0"), R.sem("p1_wld1")]
        vS = R.sem("p1_v")
        aS = R.sem("p1_a")
        tT = R.sem("p1_tT")
        tV = R.sem("p1_tV")
        tK = R.sem("p1_tK")
        tW = R.sem("p1_tW")
        tA = R.sem("p1_tA")
        kSg = [R.sem(f"p1_kS{i}") for i in range(4)]
        vsSg = [R.sem(f"p1_vs{i}") for i in range(4)]
        qSg = [R.sem(f"p1_qS{i}") for i in range(4)]
        wSg = [R.sem("p1_wS0"), R.sem("p1_wS1")]
        w2Sg = [R.sem(f"p1_w2S{i}") for i in range(4)]

        xf, xo, w1, w3, w2 = H["xf"], H["xo"], H["w1"], H["w3"], H["w2"]
        wq, wk, wv = H["wq"], H["wk"], H["wv"]
        KS, VS, QS, XNT, XFT = H["KS"], H["VS"], H["QS"], H["XNT"], H["XFT"]
        W13S, W2S = H["W13S"], H["W2S"]
        cIdB, cH128, cEps, CONST = H["cIdB"], H["cH128"], H["cEps"], H["CONST"]

        sX_free = [None, None]
        sXn_guard = [None, None]
        sXT_guard = [None, None]
        pT_free = [None, None]
        pMM_free = [None, None]
        pWT_free = [None, None]
        pSA_free = [None, None]
        sKev_free = [None] * 4
        sWld_free = [None, None]
        sWldB_free = [None, None]
        sWst_free = [None, None]
        sW2ldB_free = [None]
        sW2ev_free = [None] * 4
        sStgA_guard = [None]
        tmark = [[None] * 4, [None] * 4]
        xnt_mark = [None, None]
        state = {"prep": 0, "w2": 0}

        def prep_unit():
            pi = state["prep"]
            state["prep"] += 1
            if pi >= 128:
                return
            mat_src = w1 if pi < 64 else w3
            off = 0 if pi < 64 else D
            u = pi % 64
            s = pi % 2
            R.dma("scalar", sWld[:, s, :],
                  mat_src.rearrange("(a rr) d -> a rr d", rr=128)[:, 2 * u:2 * u + 2, :],
                  wldg[s], waits=(sWld_free[s],))
            ld = (wldg[s], wldg[s].n)
            R.op("scalar",
                 lambda e, s=s: e.activation(out=sWldB[:, s, :], in_=sWld[:, s, :],
                                             func=AF.Identity),
                 waits=(ld, sWldB_free[s]), inc=aS)
            cast_done = (aS, aS.n)
            sWld_free[s] = cast_done
            for k in range(4):
                pb = (pi * 4 + k) % 2
                for q in range(4):
                    dc = 4 * k + q
                    R.op("tensor",
                         lambda e, s=s, dc=dc, pb=pb, q=q: e.transpose(
                             pWT[pb][:, 128 * q:128 * (q + 1)],
                             sWldB[:, s, 128 * dc:128 * (dc + 1)], cIdB[:]),
                         waits=(cast_done, CONST, pWT_free[pb]) if q == 0 else (),
                         inc=tW if q == 3 else None)
                tdone = (tW, tW.n)
                R.op("vector",
                     lambda e, k=k, s=s, pb=pb: e.tensor_copy(
                         out=sWst[:, s, 512 * k:512 * (k + 1)], in_=pWT[pb][:]),
                     waits=(tdone, sWst_free[s]), inc=vS)
                pWT_free[pb] = (vS, vS.n)
            evdone = (vS, vS.n)
            sWldB_free[s] = (tW, tW.n)
            for i in range(2):
                R.dma("sync", W13S[u, 64 * i:64 * (i + 1), off:off + D],
                      sWst[64 * i:64 * (i + 1), s, :], wSg[s],
                      waits=(evdone,))
            sWst_free[s] = wSg[s]

        def w2_unit():
            wi = state["w2"]
            state["w2"] += 1
            if wi >= 16:
                return
            hh, dcb = wi // 4, wi % 4
            for j in range(4):
                dc = 4 * dcb + j
                s = j % 2
                for i in range(2):
                    R.dma("scalar", sWld[64 * i:64 * (i + 1), s, :],
                          w2[128 * dc + 64 * i:128 * dc + 64 * (i + 1),
                             2048 * hh:2048 * (hh + 1)],
                          wldg[s], waits=(sWld_free[s],) if i == 0 else ())
                ld = (wldg[s], wldg[s].n)
                R.op("scalar",
                     lambda e, s=s: e.activation(out=sW2ldB[:, 0, :],
                                                 in_=sWld[:, s, :],
                                                 func=AF.Identity),
                     waits=(ld, sW2ldB_free[0]), inc=aS)
                cast_done = (aS, aS.n)
                sWld_free[s] = cast_done
                for k in range(4):
                    pb = (wi * 16 + j * 4 + k) % 2
                    for q in range(4):
                        a = 4 * k + q
                        w = ((cast_done, CONST, pWT_free[pb], sStgA_guard[0])
                             if q == 0 else ())
                        R.op("tensor",
                             lambda e, a=a, pb=pb, q=q: e.transpose(
                                 pWT[pb][:, 128 * q:128 * (q + 1)],
                                 sW2ldB[:, 0, 128 * a:128 * (a + 1)], cIdB[:]),
                             waits=w, inc=tW if q == 3 else None)
                    tdone = (tW, tW.n)
                    R.op("vector",
                         lambda e, k=k, j=j, pb=pb: e.tensor_copy(
                             out=sStgA[:, 4 * k:4 * (k + 1),
                                       128 * j:128 * (j + 1)],
                             in_=pWT[pb][:].rearrange("p (a b) -> p a b", a=4)),
                         waits=(tdone,), inc=vS)
                    pWT_free[pb] = (vS, vS.n)
                sW2ldB_free[0] = (tW, tW.n)
            stg_ready = (vS, vS.n)
            for k in range(16):
                a = 16 * hh + k
                pb = k % 2
                R.op("tensor",
                     lambda e, k=k, pb=pb: e.matmul(
                         pSA[pb][:], cH128[:], sStgA[:, k, :],
                         start=True, stop=True),
                     waits=(stg_ready, CONST, pSA_free[pb]), inc=tA)
                mmdone = (tA, tA.n)
                eb = k % 4
                R.op("scalar",
                     lambda e, eb=eb, pb=pb: e.activation(
                         out=sW2ev[:, eb, :], in_=pSA[pb][:], func=AF.Identity,
                         scale=SCALE_Q),
                     waits=(mmdone, sW2ev_free[eb]), inc=aS)
                pSA_free[pb] = (aS, aS.n)
                ev = (aS, aS.n)
                for i in range(2):
                    R.dma("scalar",
                          W2S[32 * i:32 * (i + 1), 2 * a:2 * a + 2,
                              512 * dcb:512 * (dcb + 1)],
                          sW2ev[64 * i:64 * (i + 1), eb, :], w2Sg[eb],
                          waits=(ev,) if i == 0 else ())
                sW2ev_free[eb] = w2Sg[eb]
            sStgA_guard[0] = (tA, tA.n)

        def ln_tile_emit(src_x, g, ts, gb, tbase):
            t = tbase + 4 * g + ts
            b = t % 2
            for i in range(4):
                R.dma("sync", sX[32 * i:32 * (i + 1), b, :],
                      src_x[128 * (4 * g + ts) + 32 * i:
                            128 * (4 * g + ts) + 32 * (i + 1), :],
                      xldg[b], waits=(sX_free[b],) if i == 0 else ())
            xw = (xldg[b], xldg[b].n)
            xn_done = _ln_tile(R, vS, aS, sX[:, b, :], sXn[:, b, :],
                               sStat[:, b, :], sXn[:, b, :], xw, cEps,
                               sq_guard=(sXn_guard[b],))
            sX_free[b] = xn_done
            _xn_transposes(R, nc, sXn[:, b, :], sXT[:, gb, :, :], ts, pT,
                           pT_free, tT, vS, cIdB, xn_done, CONST,
                           (sXT_guard[gb],), t)
            sXn_guard[b] = (tT, tT.n)

        def wv_chunk(c):
            s = c % 2
            wvr = wv.rearrange("(c p) n -> p c n", p=128)
            for i in range(4):
                R.dma("scalar", sWld[32 * i:32 * (i + 1), s, :],
                      wvr[32 * i:32 * (i + 1), c, :],
                      wldg[s], waits=(sWld_free[s],) if i == 0 else ())
            ld = (wldg[s], wldg[s].n)
            R.op("scalar",
                 lambda e, c=c, s=s: e.activation(out=sW[:, c, :],
                                                  in_=sWld[:, s, :],
                                                  func=AF.Identity),
                 waits=(ld,), inc=aS)
            sWld_free[s] = (aS, aS.n)

        def load_resident(mat, ready_sem, pe_done):
            for c8 in range(8):
                R.dma("scalar", sW[:, 2 * c8:2 * (c8 + 1), :],
                      mat.rearrange("(c p) n -> p c n", p=128)[:, 2 * c8:2 * (c8 + 1), :],
                      wres, waits=(pe_done, ready_sem))
            return (wres, wres.n)

        for ts in range(4):
            ln_tile_emit(xf, 0, ts, 0, 0)
            tmark[0][ts] = (vS, vS.n)
            for c in range(4 * ts, 4 * ts + 4):
                wv_chunk(c)
        wv_rdy = (aS, aS.n)
        A_pe_g6 = None
        for g in range(8):
            gb = g % 2
            R.dma("sync", XFT[g, :, :, :], sXT[:, gb, :, :], stg[gb],
                  waits=(tmark[gb][3],))
            for vc in range(4):
                for ts4 in range(4):
                    pb = (vc * 4 + ts4) % 2
                    for dc in range(NDC):
                        R.op("tensor",
                             lambda e, gb=gb, vc=vc, ts4=ts4, dc=dc, pb=pb:
                             e.matmul(
                                 pMM[pb][:],
                                 sXT[:, gb, dc, 128 * ts4:128 * (ts4 + 1)],
                                 sW[:, dc, 512 * vc:512 * (vc + 1)],
                                 start=(dc == 0), stop=(dc == NDC - 1)),
                             waits=(tmark[gb][ts4], wv_rdy, pMM_free[pb])
                             if dc == 0 else (),
                             inc=tV if dc == NDC - 1 else None)
                    mmdone = (tV, tV.n)
                    R.op("vector",
                         lambda e, vc=vc, ts4=ts4, pb=pb: e.tensor_copy(
                             out=sVst[:, 4 * vc:4 * (vc + 1), ts4, :],
                             in_=pMM[pb][:].rearrange("p (h r) -> p h r", h=4)),
                         waits=(mmdone, vsSg[vc]), inc=vS)
                    pMM_free[pb] = (vS, vS.n)
                vgrp_done = (vS, vS.n)
                for hq in range(4):
                    h = 4 * vc + hq
                    R.dma("scalar", VS[h, :, 4 * g:4 * (g + 1), :],
                          sVst[:, h, :, :], vsSg[vc], waits=(vgrp_done,))
                if g < 7:
                    ln_tile_emit(xf, g + 1, vc, (g + 1) % 2, 0)
                    tmark[(g + 1) % 2][vc] = (vS, vS.n)
                    prep_unit()
                    prep_unit()
            if g < 4:
                prep_unit()
                prep_unit()
            sXT_guard[gb] = stg[gb]
            if g == 6:
                A_pe_g6 = (tV, tV.n)

        A_pe = (tV, tV.n)
        xft_wait = [None, None]
        xft_wait[0] = R.dma("sync", sXT[:, 0, :, :], XFT[0, :, :, :], xftg[0],
                            waits=(A_pe_g6, stg[0]))
        wk_rdy = load_resident(wk, H["WKR"], A_pe)
        sXT_B_free = [A_pe, A_pe]
        for g in range(8):
            b = g % 2
            if g + 1 < 8:
                nb = (g + 1) % 2
                xft_wait[nb] = R.dma("sync", sXT[:, nb, :, :],
                                     XFT[g + 1, :, :, :], xftg[nb],
                                     waits=(A_pe, stg[nb], sXT_B_free[nb]))
            for kc in range(NDC):
                pb = kc % 2
                for dc in range(NDC):
                    R.op("tensor",
                         lambda e, b=b, kc=kc, dc=dc, pb=pb: e.matmul(
                             pMM[pb][:], sW[:, dc, 128 * kc:128 * (kc + 1)],
                             sXT[:, b, dc, :],
                             start=(dc == 0), stop=(dc == NDC - 1)),
                         waits=(xft_wait[b], wk_rdy, pMM_free[pb])
                         if dc == 0 else (),
                         inc=tK if dc == NDC - 1 else None)
                mmdone = (tK, tK.n)
                eb = kc % 4
                R.op("scalar",
                     lambda e, eb=eb, pb=pb: e.activation(
                         out=sKev[:, eb, :], in_=pMM[pb][:], func=AF.Identity),
                     waits=(mmdone, sKev_free[eb]), inc=aS)
                pMM_free[pb] = (aS, aS.n)
                ev = (aS, aS.n)
                for i in range(2):
                    R.dma("sync",
                          KS[128 * kc + 64 * i:128 * kc + 64 * (i + 1),
                             512 * g:512 * (g + 1)],
                          sKev[64 * i:64 * (i + 1), eb, :], kSg[eb],
                          waits=(ev,) if i == 0 else ())
                sKev_free[eb] = kSg[eb]
                if kc % 2 == 1:
                    prep_unit()
            sXT_B_free[b] = (tK, tK.n)

        B_pe = (tK, tK.n)
        sXT_guard[0] = sXT_guard[1] = None
        for ts in range(4):
            ln_tile_emit(xo, 0, ts, 0, 32)
        xnt_mark[0] = (vS, vS.n)
        wq_rdy = load_resident(wq, H["WQR"], B_pe)
        for g in range(4):
            gb = g % 2
            R.dma("sync", XNT[g, :, :, :], sXT[:, gb, :, :], stg[gb],
                  waits=(xnt_mark[gb],))
            for qc in range(NDC):
                pb = qc % 2
                for dc in range(NDC):
                    R.op("tensor",
                         lambda e, gb=gb, qc=qc, dc=dc, pb=pb: e.matmul(
                             pMM[pb][:], sW[:, dc, 128 * qc:128 * (qc + 1)],
                             sXT[:, gb, dc, :],
                             start=(dc == 0), stop=(dc == NDC - 1)),
                         waits=(xnt_mark[gb], wq_rdy, pMM_free[pb])
                         if dc == 0 else (),
                         inc=tK if dc == NDC - 1 else None)
                mmdone = (tK, tK.n)
                eb = qc % 4
                R.op("scalar",
                     lambda e, eb=eb, pb=pb: e.activation(
                         out=sKev[:, eb, :], in_=pMM[pb][:], func=AF.Identity,
                         scale=SCALE_Q),
                     waits=(mmdone, sKev_free[eb]), inc=aS)
                pMM_free[pb] = (aS, aS.n)
                ev = (aS, aS.n)
                for i in range(2):
                    R.dma("sync", QS[g, 64 * i:64 * (i + 1), qc, :],
                          sKev[64 * i:64 * (i + 1), eb, :], qSg[eb],
                          waits=(ev,) if i == 0 else ())
                sKev_free[eb] = qSg[eb]
                if qc in (0, 5, 9, 13):
                    w2_unit()
                if 1 <= qc <= 4 and g < 3:
                    ln_tile_emit(xo, g + 1, qc - 1, (g + 1) % 2, 32)
                    if qc == 4:
                        xnt_mark[(g + 1) % 2] = (vS, vS.n)
            sXT_guard[gb] = stg[gb]

        while state["prep"] < 128:
            prep_unit()
        while state["w2"] < 16:
            w2_unit()

        R.flush()
        sems = (xldg + xftg + stg + kSg + qSg + w2Sg + wldg + wSg + vsSg +
                [wres, vS, aS, tT, tV, tK, tW, tA])
        return [(s, s.n) for s in sems]


def _phase3(nc, R, H):
    with ExitStack() as es:
        sAmL = es.enter_context(nc.sbuf_tensor("sAmL", [128, 4, 4, 1024], BF16)).ap()
        sS = es.enter_context(nc.sbuf_tensor("sS", [128, 4, 4096], F32)).ap()
        sP = es.enter_context(nc.sbuf_tensor("sP", [128, 2, 4, 4096], BF16)).ap()
        sPT = es.enter_context(nc.sbuf_tensor("sPT", [128, 4, 512], BF16)).ap()
        sK = es.enter_context(nc.sbuf_tensor("sK", [128, 2, 4096], BF16)).ap()
        sV = es.enter_context(nc.sbuf_tensor("sV", [128, 2, 32, DH], BF16)).ap()
        sQc = es.enter_context(nc.sbuf_tensor("sQc", [128, 2, 512], BF16)).ap()
        sCM = es.enter_context(nc.sbuf_tensor("sCM", [128, 4, 4], F32)).ap()
        sNM = es.enter_context(nc.sbuf_tensor("sNM", [128, 2, 4], F32)).ap()
        sL = es.enter_context(nc.sbuf_tensor("sL", [128, 2, 4], F32)).ap()
        sRi = es.enter_context(nc.sbuf_tensor("sRi", [128, 2, 4], F32)).ap()
        sA = es.enter_context(nc.sbuf_tensor("sA", [128, 2, 512], BF16)).ap()
        pS = [es.enter_context(nc.psum_tensor(f"p3pS{i}", [128, 2, 512], F32)).ap()
              for i in range(2)]
        pTT = es.enter_context(nc.psum_tensor("p3pTT", [128, 2, 512], BF16)).ap()
        pO = [es.enter_context(nc.psum_tensor(f"p3pO{i}", [128, 512], F32)).ap()
              for i in range(2)]

        dLg = [R.sem("p3_dL0"), R.sem("p3_dL1")]
        mld = R.sem("p3_mld")
        dAg = [R.sem("p3_dA0"), R.sem("p3_dA1")]
        tQK = R.sem("p3_tQK")
        tTP = R.sem("p3_tTP")
        tPV = R.sem("p3_tPV")
        aSE = R.sem("p3_aSE")
        aPT = R.sem("p3_aPT")
        aE = R.sem("p3_aE")
        aA = R.sem("p3_aA")
        vSE = R.sem("p3_vSE")
        vSum = R.sem("p3_vSum")
        gP = R.sem("p3_gP")
        gM = R.sem("p3_gM")

        KS, VS, QS, AS, amask = H["KS"], H["VS"], H["QS"], H["AS"], H["amask"]
        CONST = H["CONST"]

        R.dma("sync", sAmL[:, :, :, :],
              amask.rearrange("l i p w -> p l i w"), mld)
        MASKS = (mld, mld.n)
        ld_free = [None, None]
        pS_free = [None, None]
        pTT_free = [None]
        pO_free = [None, None]
        sA_free = [None]
        sPT_base = [0]
        sNM_guard = [None, None]
        sL_guard = [None, None]
        sRi_guard = [None, None]
        state = {}
        units = [(l, h) for h in range(NH) for l in range(4)]

        def deferred(p):
            l, h, kv, nb, n5, par, _exp_done, scale_done, _tTf = state[p]
            ev_jb = {}
            pv_base = [None]

            def emit_Tpair(jp):
                for sub in range(2):
                    jb = 2 * jp + sub
                    for i2 in range(4):
                        R.op("tensor",
                             lambda e, jb=jb, i2=i2, sub=sub, par=par: e.transpose(
                                 pTT[:, sub, 128 * i2:128 * (i2 + 1)],
                                 sP[:, par, i2, 128 * jb:128 * (jb + 1)],
                                 H["cIdB"][:]),
                             waits=(scale_done, CONST, pTT_free[0])
                             if (sub == 0 and i2 == 0) else (),
                             inc=tTP if (sub == 1 and i2 == 3) else None)
                tdone = (tTP, tTP.n)
                slot = (2 * jp) % 4
                w = [tdone]
                if pv_base[0] is None:
                    pv_base[0] = tPV.n
                if jp >= 2:
                    w.append((tPV, pv_base[0] + 2 * jp - 2))
                if jp % 2 == 0:
                    R.op("vector",
                         lambda e, jp=jp, slot=slot: e.tensor_copy(
                             out=sPT[:, slot:slot + 2, :], in_=pTT[:, :, :]),
                         waits=tuple(w), inc=vSE)
                    ev_jb[jp] = (vSE, vSE.n)
                else:
                    R.op("scalar",
                         lambda e, jp=jp, slot=slot: e.activation(
                             out=sPT[:, slot:slot + 2, :], in_=pTT[:, :, :],
                             func=AF.Identity),
                         waits=tuple(w), inc=aPT)
                    ev_jb[jp] = (aPT, aPT.n)
                pTT_free[0] = ev_jb[jp]

            def emit_PV(jb):
                w = [ev_jb[jb // 2]]
                if jb == 0:
                    w.append(pO_free[par])
                R.op("tensor",
                     lambda e, par=par, jb=jb, nb=nb: e.matmul(
                         pO[par][:], sV[:, par, jb, :], sPT[:, jb % 4, :],
                         start=(jb == 0), stop=(jb == nb - 1)),
                     waits=tuple(w), inc=tPV)

            steps = [lambda: emit_Tpair(0)]
            for jp in range(1, nb // 2):
                steps.append(lambda jp=jp: (emit_Tpair(jp), emit_PV(2 * jp - 2),
                                            emit_PV(2 * jp - 1)))
            steps.append(lambda: (emit_PV(nb - 2), emit_PV(nb - 1)))

            def tail():
                pv_done = (tPV, tPV.n)
                ld_free[par] = pv_done
                R.op("scalar",
                     lambda e, par=par: e.activation(
                         out=sA[:, 0, :], in_=pO[par][:], func=AF.Identity),
                     waits=(pv_done, sA_free[0]), inc=aA)
                pO_free[par] = (aA, aA.n)
                R.dma("scalar", AS[h, :, 512 * l:512 * (l + 1)], sA[:, 0, :],
                      dAg[par], waits=((aA, aA.n),))
                sA_free[0] = dAg[par]
            return steps, tail

        loads_mark = {}

        def emit_loads(j):
            if j >= len(units):
                return
            lj, hj = units[j]
            kvj = KVS[lj]
            nbj = kvj // 128
            pj = j % 2
            R.dma("sync", sK[:, pj, 0:kvj],
                  KS[128 * hj:128 * (hj + 1), 0:kvj],
                  dLg[pj], waits=(ld_free[pj],))
            R.dma("sync", sV[:, pj, 0:nbj, :], VS[hj, :, 0:nbj, :], dLg[pj])
            R.dma("sync", sQc[:, pj, :], QS[lj, :, hj, :], dLg[pj])
            loads_mark[j] = (dLg[pj], dLg[pj].n)

        emit_loads(0)
        for idx, (l, h) in enumerate(units):
            kv = KVS[l]
            nb = kv // 128
            n5 = kv // 512
            par = idx % 2
            mask_ready = MASKS
            lds = loads_mark[idx]

            prev_exp_rows = state[idx - 1][6] if idx > 0 else [None] * 4
            pexp_guard = state[idx - 1][8] if idx > 1 else None
            row_started = [False] * 4
            exp_rows = [None] * 4
            bcnt = [0]
            nbat = n5 // 2

            def emit_qk_batch(i2, bi, idx=idx, l=l, kv=kv, n5=n5, nbat=nbat,
                              par=par, lds=lds, mask_ready=mask_ready,
                              prev_exp_rows=prev_exp_rows,
                              pexp_guard=pexp_guard):
                pp = bcnt[0] % 2
                for k in range(2):
                    j5 = 2 * bi + k
                    R.op("tensor",
                         lambda e, par=par, i2=i2, j5=j5, k=k, pp=pp: e.matmul(
                             pS[pp][:, k, :],
                             sQc[:, par, 128 * i2:128 * (i2 + 1)],
                             sK[:, par, 512 * j5:512 * (j5 + 1)],
                             start=True, stop=True),
                         waits=(lds, pS_free[pp]) if k == 0 else (),
                         inc=tQK if k == 1 else None)
                mmdone = (tQK, tQK.n)
                j0 = 2 * bi
                w = [mmdone]
                if not row_started[i2]:
                    w.append(prev_exp_rows[i2])
                    row_started[i2] = True
                if bi == nbat - 1:
                    w.append(mask_ready)
                    R.op("vector",
                         lambda e, i2=i2, j0=j0, pp=pp, l=l: e.tensor_tensor(
                             out=sS[:, i2, 512 * j0:512 * (j0 + 2)]
                             .rearrange("p (a b) -> p a b", a=2),
                             in0=pS[pp][:, :, :],
                             in1=sAmL[:, l, i2, :]
                             .rearrange("p (a b) -> p a b", a=2),
                             op=ALU.add),
                         waits=tuple(w), inc=vSE)
                    ev = (vSE, vSE.n)
                elif bcnt[0] % 4 == 3:
                    R.op("vector",
                         lambda e, i2=i2, j0=j0, pp=pp: e.tensor_copy(
                             out=sS[:, i2, 512 * j0:512 * (j0 + 2)],
                             in_=pS[pp][:, :, :]),
                         waits=tuple(w), inc=vSE)
                    ev = (vSE, vSE.n)
                else:
                    R.op("scalar",
                         lambda e, i2=i2, j0=j0, pp=pp: e.activation(
                             out=sS[:, i2, 512 * j0:512 * (j0 + 2)],
                             in_=pS[pp][:, :, :], func=AF.Identity),
                         waits=tuple(w), inc=aSE)
                    ev = (aSE, aSE.n)
                pS_free[pp] = ev
                R.op("vector",
                     lambda e, i2=i2, bi=bi, j0=j0: e.reduce_max(
                         sCM[:, i2, bi:bi + 1],
                         sS[:, i2, 512 * j0:512 * (j0 + 2)], axis=AX.X),
                     waits=(ev,), inc=vSE)
                bcnt[0] += 1
                if bi == nbat - 1:
                    R.op("vector",
                         lambda e, i2=i2, nbat=nbat, par=par: e.reduce_max(
                             sNM[:, par, i2:i2 + 1], sCM[:, i2, 0:nbat],
                             axis=AX.X, negate=True),
                         waits=((vSE, vSE.n), sNM_guard[par]), inc=vSE)
                    nm = (vSE, vSE.n)
                    R.op("scalar",
                         lambda e, i2=i2, kv=kv, par=par: e.activation(
                             out=sP[:, par, i2, 0:kv], in_=sS[:, i2, 0:kv],
                             func=AF.Exp, bias=sNM[:, par, i2:i2 + 1],
                             accum_out=sL[:, par, i2:i2 + 1]),
                         waits=(nm, sL_guard[par], pexp_guard), inc=aE)
                    exp_rows[i2] = (aE, aE.n)
                    R.op("vector",
                         lambda e, i2=i2, par=par: e.reciprocal(
                             sRi[:, par, i2:i2 + 1], sL[:, par, i2:i2 + 1]),
                         waits=((aE, aE.n), sRi_guard[par]), inc=vSum)
                    R.op("vector",
                         lambda e, i2=i2, kv=kv, par=par: e.tensor_scalar_mul(
                             sP[:, par, i2, 0:kv], sP[:, par, i2, 0:kv],
                             sRi[:, par, i2:i2 + 1]),
                         waits=((vSum, vSum.n),), inc=gP)

            qk_steps = [(i2, bi) for i2 in range(4) for bi in range(nbat)]
            if idx > 0:
                pair_steps, def_tail = deferred(idx - 1)
            else:
                pair_steps, def_tail = [], None
            nq, nd = len(qk_steps), len(pair_steps)
            qi = di = 0
            while qi < nq or di < nd:
                take_q = 2 if qi == 0 else 1
                for _ in range(take_q):
                    if qi < nq:
                        emit_qk_batch(*qk_steps[qi])
                        qi += 1
                if nq - qi <= 0:
                    while di < nd:
                        pair_steps[di]()
                        di += 1
                else:
                    want = int(round((qi / nq) * nd))
                    while di < min(want, nd):
                        pair_steps[di]()
                        di += 1
            if def_tail is not None:
                def_tail()
            emit_loads(idx + 1)

            sNM_guard[par] = (aE, aE.n)
            sL_guard[par] = (vSum, vSum.n)
            sRi_guard[par] = (gP, gP.n)
            state[idx] = (l, h, kv, nb, n5, par, exp_rows, (gP, gP.n),
                          (tTP, tTP.n))
        last_steps, last_tail = deferred(len(units) - 1)
        for st_fn in last_steps:
            st_fn()
        last_tail()

        R.flush()
        sems = dLg + dAg + [mld, tQK, tTP, tPV, aSE, aPT, aE, aA, vSE,
                            vSum, gP]
        return [(s, s.n) for s in sems]


def _phase4(nc, R, H):
    with ExitStack() as es:
        sH2 = es.enter_context(nc.sbuf_tensor("sH2", [128, NU, 512], BF16)).ap()
        sXnTc = es.enter_context(nc.sbuf_tensor("sXnTc", [128, NDC, 512], BF16)).ap()
        sW13t = es.enter_context(nc.sbuf_tensor("sW13t", [128, 3, 2 * D], BF16)).ap()
        sW2t = es.enter_context(nc.sbuf_tensor("sW2t", [128, 4, 1024], BF16)).ap()
        sWoT = es.enter_context(nc.sbuf_tensor("sWoT", [128, 2, 1024], BF16)).ap()
        sAt = es.enter_context(nc.sbuf_tensor("sAt", [128, 2, 512], BF16)).ap()
        sSil = es.enter_context(nc.sbuf_tensor("sSil", [128, 2, 512], F32)).ap()
        sHb = es.enter_context(nc.sbuf_tensor("sHb", [128, 2, 512], BF16)).ap()
        sFM = es.enter_context(nc.sbuf_tensor("sFM", [128, NDC, 512], F32)).ap()
        sXo = es.enter_context(nc.sbuf_tensor("p4sXo", [128, 2, D], F32)).ap()
        sOutT = es.enter_context(nc.sbuf_tensor("sOutT", [128, 2, D], F32)).ap()

        xnl = R.sem("p4_xnl")
        wlg = [R.sem("p4_wl0"), R.sem("p4_wl1"), R.sem("p4_wl2")]
        w2tg = [R.sem(f"p4_w2t{i}") for i in range(4)]
        atg = [R.sem("p4_at0"), R.sem("p4_at1")]
        glg = [R.sem("p4_gl0"), R.sem("p4_gl1")]
        xlg = [R.sem("p4_xl0"), R.sem("p4_xl1")]
        oSg = [R.sem("p4_oS0"), R.sem("p4_oS1")]
        tU = R.sem("p4_tU")
        tL = R.sem("p4_tL")
        tBB = R.sem("p4_tB")
        tG = R.sem("p4_tG")
        tOT = R.sem("p4_tOT")
        aSi = R.sem("p4_aSi")
        aH2 = R.sem("p4_aH2")
        aG = R.sem("p4_aG")
        vMu = R.sem("p4_vMu")
        vO = R.sem("p4_vO")

        xo, wout = H["xo"], H["wout"]
        XNT, AS, W13S, W2S = H["XNT"], H["AS"], H["W13S"], H["W2S"]
        cH64, cIdF, out, CONST = H["cH64"], H["cIdF"], H["out"], H["CONST"]

        sW13_free = [None, None, None]
        sW2_free = [None, None, None, None]
        sWo_free = [None, None]
        sAt_free = [None, None]
        sSil_free = [None, None]
        sHb_free = [None, None]
        sXo_free = [None, None]
        sOutT_free = [None, None]
        sH2_guard = [None]
        sXnTc_guard = [None]
        sFM_guard = [None]
        bank_guard = [None]

        for c in range(4):
            R.dma("sync", sXnTc[:, :, :], XNT[c, :, :, :], xnl,
                  waits=(sXnTc_guard[0],))
            xnt_ready = (xnl, xnl.n)

            with ExitStack() as es_b:
                pUp = [es_b.enter_context(
                    nc.psum_tensor(f"p4pUp{c}{i}", [128, 512], F32)).ap()
                    for i in range(2)]
                pLin = [es_b.enter_context(
                    nc.psum_tensor(f"p4pLin{c}{i}", [128, 512], F32)).ap()
                    for i in range(2)]
                pB = [es_b.enter_context(
                    nc.psum_tensor(f"p4pB{c}{i}", [128, 512], F32)).ap()
                    for i in range(2)]
                pUp_free = [bank_guard[0], bank_guard[0]]
                pLin_free = [bank_guard[0], bank_guard[0]]
                pB_free = [bank_guard[0], bank_guard[0]]
                mul_done = {}

                def stage_b(u):
                    b = u % 2
                    R.op("tensor",
                         lambda e, b=b: e.matmul(
                             pB[b][:], cH64[:], sHb[:, b, :],
                             start=True, stop=True),
                         waits=(mul_done[u], CONST, pB_free[b], sH2_guard[0]),
                         inc=tBB)
                    sHb_free[b] = (tBB, tBB.n)
                    R.op("scalar",
                         lambda e, b=b, u=u: e.activation(
                             out=sH2[:, u, :], in_=pB[b][:], func=AF.Copy,
                             scale=0.125),
                         waits=((tBB, tBB.n),), inc=aH2)
                    pB_free[b] = (aH2, aH2.n)

                for u in range(NU):
                    b = u % 2
                    b3 = u % 3
                    R.dma("sync", sW13t[:, b3, :], W13S[u, :, :], wlg[b3],
                          waits=(sW13_free[b3],))
                    wld = (wlg[b3], wlg[b3].n)
                    for dc in range(NDC):
                        R.op("tensor",
                             lambda e, b=b, b3=b3, dc=dc: e.matmul(
                                 pUp[b][:], sW13t[:, b3, 128 * dc:128 * (dc + 1)],
                                 sXnTc[:, dc, :], start=(dc == 0),
                                 stop=(dc == NDC - 1)),
                             waits=(wld, xnt_ready, pUp_free[b]) if dc == 0 else (),
                             inc=tU if dc == NDC - 1 else None)
                    for dc in range(NDC):
                        R.op("tensor",
                             lambda e, b=b, b3=b3, dc=dc: e.matmul(
                                 pLin[b][:], sW13t[:, b3, D + 128 * dc:D + 128 * (dc + 1)],
                                 sXnTc[:, dc, :], start=(dc == 0),
                                 stop=(dc == NDC - 1)),
                             waits=(pLin_free[b],) if dc == 0 else (),
                             inc=tL if dc == NDC - 1 else None)
                    sW13_free[b3] = (tL, tL.n)
                    if u > 0:
                        stage_b(u - 1)
                    R.op("scalar",
                         lambda e, b=b: e.activation(
                             out=sSil[:, b, :], in_=pUp[b][:], func=AF.Silu),
                         waits=((tU, tU.n), sSil_free[b]), inc=aSi)
                    pUp_free[b] = (aSi, aSi.n)
                    R.op("vector",
                         lambda e, b=b: e.tensor_tensor(
                             out=sHb[:, b, :], in0=pLin[b][:],
                             in1=sSil[:, b, :], op=ALU.mult),
                         waits=((tL, tL.n), (aSi, aSi.n), sHb_free[b]), inc=vMu)
                    pLin_free[b] = (vMu, vMu.n)
                    sSil_free[b] = (vMu, vMu.n)
                    mul_done[u] = (vMu, vMu.n)
                stage_b(NU - 1)
                h2_ready = (aH2, aH2.n)
                sXnTc_guard[0] = (tL, tL.n)
                R.flush()

            with ExitStack() as es_c:
                pG = es_c.enter_context(
                    nc.psum_tensor(f"p4pG{c}", [128, 8, 512], F32)).ap()
                pG_free = (aH2, aH2.n)
                for half in range(2):
                    for u in range(NU):
                        b = (half * NU + u) % 4
                        R.dma("scalar", sW2t[:, b, :],
                              W2S[u, :, 1024 * half:1024 * (half + 1)], w2tg[b],
                              waits=(sW2_free[b],))
                        wld = (w2tg[b], w2tg[b].n)
                        for d8 in range(8):
                            w = ()
                            if u == 0:
                                w = (wld, h2_ready, sFM_guard[0], pG_free)
                            elif d8 == 0:
                                w = (wld,)
                            R.op("tensor",
                                 lambda e, b=b, d8=d8, u=u: e.matmul(
                                     pG[:, d8, :],
                                     sW2t[:, b, 128 * d8:128 * (d8 + 1)],
                                     sH2[:, u, :], start=(u == 0), stop=False),
                                 waits=w, inc=tG if d8 == 7 else None)
                        sW2_free[b] = (tG, tG.n)
                    for cc in range(NDC):
                        b = cc % 2
                        R.dma("scalar", sWoT[:, b, :],
                              wout[128 * cc:128 * (cc + 1),
                                   1024 * half:1024 * (half + 1)],
                              glg[b], waits=(sWo_free[b], H["WOR"]))
                        R.dma("scalar", sAt[:, b, :],
                              AS[cc, :, 512 * c:512 * (c + 1)], atg[b],
                              waits=(sAt_free[b],))
                        wld2 = ((glg[b], glg[b].n), (atg[b], atg[b].n))
                        for d8 in range(8):
                            R.op("tensor",
                                 lambda e, b=b, d8=d8, cc=cc: e.matmul(
                                     pG[:, d8, :],
                                     sWoT[:, b, 128 * d8:128 * (d8 + 1)],
                                     sAt[:, b, :], start=False,
                                     stop=(cc == NDC - 1)),
                                 waits=wld2 if d8 == 0 else (),
                                 inc=tG if d8 == 7 else None)
                        sWo_free[b] = (tG, tG.n)
                        sAt_free[b] = (tG, tG.n)
                    grp_done = (tG, tG.n)
                    R.op("scalar",
                         lambda e, half=half: e.activation(
                             out=sFM[:, 8 * half:8 * (half + 1), :],
                             in_=pG[:, :, :], func=AF.Copy),
                         waits=(grp_done,), inc=aG)
                    pG_free = (aG, aG.n)
                sH2_guard[0] = (tG, tG.n)
                fm_ready = (aG, aG.n)
                R.flush()

            with ExitStack() as es_d:
                pOq = es_d.enter_context(
                    nc.psum_tensor(f"p4pOq{c}", [128, 4, 512], F32)).ap()
                pOq_free = [(aG, aG.n)] * 4
                for ts in range(4):
                    b = ts % 2
                    t = 4 * c + ts
                    R.dma("sync", sXo[:, b, :], xo[128 * t:128 * (t + 1), :],
                          xlg[b], waits=(sXo_free[b],))
                    xw = (xlg[b], xlg[b].n)
                    for dc in range(NDC):
                        qb = dc // 4
                        R.op("tensor",
                             lambda e, dc=dc, ts=ts, qb=qb: e.transpose(
                                 pOq[:, qb, 128 * (dc % 4):128 * (dc % 4 + 1)],
                                 sFM[:, dc, 128 * ts:128 * (ts + 1)], cIdF[:]),
                             waits=(fm_ready, CONST, pOq_free[qb])
                             if dc % 4 == 0 else (),
                             inc=tOT if dc % 4 == 3 else None)
                    tdone = (tOT, tOT.n)
                    for qb in range(4):
                        R.op("vector",
                             lambda e, qb=qb, b=b: e.tensor_tensor(
                                 out=sOutT[:, b, 512 * qb:512 * (qb + 1)],
                                 in0=pOq[:, qb, :],
                                 in1=sXo[:, b, 512 * qb:512 * (qb + 1)],
                                 op=ALU.add),
                             waits=(tdone, xw, sOutT_free[b]), inc=vO)
                        pOq_free[qb] = (vO, vO.n)
                    sXo_free[b] = (vO, vO.n)
                    R.dma("sync", out[128 * t:128 * (t + 1), :], sOutT[:, b, :],
                          oSg[b], waits=((vO, vO.n),))
                    sOutT_free[b] = oSg[b]
                sFM_guard[0] = (tOT, tOT.n)
                bank_guard[0] = (vO, vO.n)
                R.flush()

        return oSg


def kernel(L0_attn_x, L0_attn_w_qkv, L0_attn_w_out, L0_ffn_W1, L0_ffn_W2,
           L0_ffn_W3):
    x = np.asarray(L0_attn_x, dtype=np.float32)
    wqkv = np.asarray(L0_attn_w_qkv, dtype=np.float32)
    wq = np.ascontiguousarray(wqkv[:, 0:D])
    wk = np.ascontiguousarray(wqkv[:, D:2 * D])
    wv = np.ascontiguousarray(wqkv[:, 2 * D:3 * D])
    wout = np.ascontiguousarray(np.asarray(L0_attn_w_out, dtype=np.float32))
    w1 = np.ascontiguousarray(np.asarray(L0_ffn_W1, dtype=np.float32))
    w2 = np.ascontiguousarray(np.asarray(L0_ffn_W2, dtype=np.float32))
    w3 = np.ascontiguousarray(np.asarray(L0_ffn_W3, dtype=np.float32))

    h128c = _hadamard(128)
    h64c = _h64_i2()
    identf = np.eye(128, dtype=np.float32)
    masks = [np.ascontiguousarray(_amask(j)) for j in (0, 1)]

    nc = build_program()
    in_maps = []
    for c in range(8):
        b, j = c // 2, c % 2
        xb = np.ascontiguousarray(x[b])
        xov = np.ascontiguousarray(
            x[b].reshape(8, 512, D)[S_SETS[j]].reshape(OWN, D))
        in_maps.append(dict(xf=xb, xo=xov, wq=wq, wk=wk, wv=wv, wout=wout,
                            w1=w1, w3=w3, w2=w2, amask=masks[j], h128=h128c,
                            h64=h64c, identf=identf))
    res = run_bass_kernel_spmd(nc, in_maps, core_ids=list(range(8)))
    outa = np.empty((4, SEQ, D), dtype=np.float32)
    for c in range(8):
        b, j = c // 2, c % 2
        oc = res.results[c]["out"].reshape(4, 512, D)
        for l, g in enumerate(S_SETS[j]):
            outa[b, 512 * g:512 * (g + 1)] = oc[l]
    return outa


# version 23
# speedup vs baseline: 1.0826x; 1.0826x over previous
import numpy as np
import ml_dtypes
import concourse.bass as bass
import concourse.mybir as mybir
from concourse.bass_utils import run_bass_kernel_spmd
from contextlib import ExitStack

F32 = mybir.dt.float32
BF16 = mybir.dt.bfloat16
AF = mybir.ActivationFunctionType
ALU = mybir.AluOpType
AX = mybir.AxisListType

D = 2048
SEQ = 4096
OWN = 2048
NH = 16
DH = 128
FH = 8192
NDC = 16
NU = 64
SCALE_Q = 1.0 / float(np.sqrt(128.0))
S_SETS = ([0, 3, 4, 7], [1, 2, 5, 6])
KVS = [1024, 2048, 3072, 4096]
NEG = -1.0e30

ENGINES = ("tensor", "vector", "scalar", "gpsimd", "sync")


class Sem:
    __slots__ = ("h", "n", "name")

    def __init__(self, h, name):
        self.h = h
        self.n = 0
        self.name = name


class Rec:

    def __init__(self, nc, blk):
        self.nc = nc
        self.blk = blk
        self.ops = {e: [] for e in ENGINES}
        self.wm = {e: {} for e in ENGINES}

    def sem(self, name):
        return Sem(self.nc.alloc_semaphore(name), name)

    def op(self, eng, fn, waits=(), inc=None, inc_by=1):
        lst = self.ops[eng]
        for w in waits:
            if w is None:
                continue
            if isinstance(w, Sem):
                s, v = w, w.n
            else:
                s, v = w
            if v is None or v <= 0:
                continue
            if self.wm[eng].get(id(s), -1) < v:
                lst.append(("w", s, v))
                self.wm[eng][id(s)] = v
        if inc is not None:
            inc.n += inc_by
        if fn is not None:
            lst.append(("x", fn, inc, inc_by))
        return (inc, inc.n) if inc is not None else None

    def dma(self, q, out, in_, sem, waits=()):
        self.op(q, (lambda e, o=out, i=in_: e.dma_start(out=o, in_=i)),
                waits, inc=sem, inc_by=16)
        return (sem, sem.n)

    def dmaT(self, q, out, in_, sem, waits=()):
        self.op(q, (lambda e, o=out, i=in_: e.dma_start(out=o, in_=i,
                                                        transpose=True)),
                waits, inc=sem, inc_by=16)
        return (sem, sem.n)

    def fence(self, fences):
        for e in ENGINES:
            for (s, v) in fences:
                if v > 0 and self.wm[e].get(id(s), -1) < v:
                    self.ops[e].append(("w", s, v))
                    self.wm[e][id(s)] = v

    def flush(self):
        for e in ENGINES:
            lst = self.ops[e]
            if not lst:
                continue

            def body(eng, lst=tuple(lst)):
                for item in lst:
                    if item[0] == "w":
                        eng.wait_ge(item[1].h, item[2])
                    else:
                        _, fn, inc, k = item
                        ins = fn(eng)
                        if inc is not None:
                            ins.then_inc(inc.h, k)
            getattr(self.blk, e)(body)
            self.ops[e] = []


def _popcount_and(n):
    i = np.arange(n)
    pc = np.zeros((n, n), dtype=np.int64)
    a = i[:, None] & i[None, :]
    while np.any(a):
        pc += a & 1
        a >>= 1
    return pc


def _hadamard(n):
    return np.where(_popcount_and(n) % 2 == 0, 1.0, -1.0).astype(np.float32)


def _h64_i2():
    h64 = _hadamard(64)
    m = np.zeros((128, 128), dtype=np.float32)
    for r2 in range(2):
        m[r2::2, r2::2] = h64
    return m


def _amask(j):
    m = np.zeros((4, 4, 128, 1024), dtype=np.float32)
    for l in range(4):
        g = S_SETS[j][l]
        kvs = KVS[l]
        for i2 in range(4):
            qpos = g * 512 + i2 * 128 + np.arange(128)[:, None]
            jabs = kvs - 1024 + np.arange(1024)[None, :]
            m[l, i2] = np.where(jabs <= qpos, 0.0, NEG)
    return m.astype(ml_dtypes.bfloat16)


def _ln_tile(R, vS, aS, sX, sXsq, st, sXn, xld_wait, cEps, sq_guard=()):
    R.op("vector", lambda e, sX=sX, st=st: e.reduce_sum(st[:, 0:1], sX, axis=AX.X),
         waits=(xld_wait,), inc=vS)
    R.op("vector", lambda e, st=st: e.tensor_scalar_mul(st[:, 1:2], st[:, 0:1],
                                                        1.0 / D),
         waits=((vS, vS.n),), inc=vS)
    R.op("scalar", lambda e, sX=sX, st=st: e.activation(
        out=sXsq, in_=sX, func=AF.Square, accum_out=st[:, 2:3]),
        waits=(xld_wait,) + tuple(sq_guard), inc=aS)
    asq = (aS, aS.n)
    R.op("vector", lambda e, st=st: e.tensor_mul(st[:, 4:5], st[:, 1:2], st[:, 1:2]),
         waits=((vS, vS.n),), inc=vS)
    R.op("vector", lambda e, st=st: e.scalar_tensor_tensor(
        out=st[:, 3:4], in0=st[:, 2:3], scalar=1.0 / D, in1=st[:, 4:5],
        op0=ALU.mult, op1=ALU.subtract), waits=(asq, (vS, vS.n)), inc=vS)
    vvar = (vS, vS.n)
    R.op("scalar", lambda e, st=st, cEps=cEps: e.activation(
        out=st[:, 5:6], in_=st[:, 3:4], func=AF.Sqrt, bias=cEps[:, 0:1]),
        waits=(vvar,), inc=aS)
    asd = (aS, aS.n)
    R.op("vector", lambda e, st=st: e.reciprocal(st[:, 6:7], st[:, 5:6]),
         waits=(asd,), inc=vS)
    R.op("vector", lambda e, sX=sX, st=st, sXn=sXn: e.tensor_scalar(
        out=sXn, in0=sX, scalar1=st[:, 1:2], scalar2=st[:, 6:7],
        op0=ALU.subtract, op1=ALU.mult), waits=((vS, vS.n),), inc=vS)
    return (vS, vS.n)


def _xn_transposes(R, nc_t, sXn, sXnT, ts, pT, pT_free, tT, vS, cIdB,
                   xn_done, CONST, extra_guard, tkey):
    for k in range(4):
        pb = (tkey * 4 + k) % 2
        for q in range(4):
            dc = 4 * k + q
            w = (xn_done, CONST, pT_free[pb]) + tuple(extra_guard) if q == 0 else ()
            R.op("tensor",
                 lambda e, sXn=sXn, dc=dc, pb=pb, q=q: e.transpose(
                     pT[pb][:, 128 * q:128 * (q + 1)],
                     sXn[:, 128 * dc:128 * (dc + 1)], cIdB[:]),
                 waits=w, inc=tT if q == 3 else None)
        tdone = (tT, tT.n)
        R.op("vector",
             lambda e, k=k, ts=ts, pb=pb: e.tensor_copy(
                 out=sXnT[:, 4 * k:4 * (k + 1), 128 * ts:128 * (ts + 1)],
                 in_=pT[pb][:].rearrange("p (a b) -> p a b", a=4)),
             waits=(tdone,), inc=vS)
        pT_free[pb] = (vS, vS.n)


def build_program(debug=False):
    nc = bass.Bass("TRN2", target_bir_lowering=False, debug=False)

    xf = nc.dram_tensor("xf", [SEQ, D], F32, kind="ExternalInput").ap()
    xo = nc.dram_tensor("xo", [OWN, D], F32, kind="ExternalInput").ap()
    wq = nc.dram_tensor("wq", [D, D], F32, kind="ExternalInput").ap()
    wk = nc.dram_tensor("wk", [D, D], F32, kind="ExternalInput").ap()
    wv = nc.dram_tensor("wv", [D, D], F32, kind="ExternalInput").ap()
    wout = nc.dram_tensor("wout", [D, D], F32, kind="ExternalInput").ap()
    w1 = nc.dram_tensor("w1", [FH, D], F32, kind="ExternalInput").ap()
    w3 = nc.dram_tensor("w3", [FH, D], F32, kind="ExternalInput").ap()
    w2 = nc.dram_tensor("w2", [D, FH], F32, kind="ExternalInput").ap()
    amask = nc.dram_tensor("amask", [4, 4, 128, 1024], BF16, kind="ExternalInput").ap()
    h128 = nc.dram_tensor("h128", [128, 128], F32, kind="ExternalInput").ap()
    h64 = nc.dram_tensor("h64", [128, 128], F32, kind="ExternalInput").ap()
    identf = nc.dram_tensor("identf", [128, 128], F32, kind="ExternalInput").ap()
    out = nc.dram_tensor("out", [OWN, D], F32, kind="ExternalOutput").ap()

    WQB = nc.dram_tensor("WQB", [D, D], BF16, kind="Internal").ap()
    WKB = nc.dram_tensor("WKB", [D, D], BF16, kind="Internal").ap()
    WOB = nc.dram_tensor("WOB", [D, D], BF16, kind="Internal").ap()
    KS = nc.dram_tensor("KS", [D, SEQ], BF16, kind="Internal").ap()
    VS = nc.dram_tensor("VS", [NH, 128, 32, DH], BF16, kind="Internal").ap()
    QS = nc.dram_tensor("QS", [4, 128, NH, 512], BF16, kind="Internal").ap()
    XNT = nc.dram_tensor("XNT", [4, 128, NDC, 512], BF16, kind="Internal").ap()
    XFT = nc.dram_tensor("XFT", [8, 128, NDC, 512], BF16, kind="Internal").ap()
    AS = nc.dram_tensor("AS", [NH, 128, OWN], BF16, kind="Internal").ap()
    W13S = nc.dram_tensor("W13S", [NU, 128, 2 * D], BF16, kind="Internal").ap()
    W2S = nc.dram_tensor("W2S", [NU, 128, D], BF16, kind="Internal").ap()

    dbg = {}
    if debug:
        for name, t, shape in (("dKS", KS, [D, SEQ]), ("dVS", VS, [NH, 128, 32, DH]),
                               ("dQS", QS, [4, 128, NH, 512]),
                               ("dXNT", XNT, [4, 128, NDC, 512]),
                               ("dAS", AS, [NH, 128, OWN]),
                               ("dW13S", W13S, [NU, 128, 2 * D]),
                               ("dW2S", W2S, [NU, 128, D])):
            dbg[name] = (nc.dram_tensor(name, shape, BF16, kind="ExternalOutput").ap(), t)

    with ExitStack() as es_all:
        cIdB = es_all.enter_context(nc.sbuf_tensor("cIdB", [128, 128], BF16)).ap()
        cIdF = es_all.enter_context(nc.sbuf_tensor("cIdF", [128, 128], F32)).ap()
        cH128 = es_all.enter_context(nc.sbuf_tensor("cH128", [128, 128], BF16)).ap()
        cH64 = es_all.enter_context(nc.sbuf_tensor("cH64", [128, 128], BF16)).ap()
        cOnesF = es_all.enter_context(nc.sbuf_tensor("cOnesF", [1, 128], F32)).ap()
        cEps = es_all.enter_context(nc.sbuf_tensor("cEps", [128, 1], F32)).ap()

        blk = es_all.enter_context(nc.Block())
        R = Rec(nc, blk)

        cst = R.sem("cst")
        cstf = R.sem("cstf")
        R.dma("gpsimd", cIdB[:], identf[:], cst)
        R.dma("sync", cIdF[:], identf[:], cstf)
        R.dma("gpsimd", cH128[:], h128[:], cst)
        R.dma("gpsimd", cH64[:], h64[:], cst)
        CONST = (cst, cst.n)
        vcst = R.sem("vcst")
        R.op("vector", lambda e: e.memset(cOnesF[:], 1.0), inc=vcst)
        R.op("vector", lambda e: e.memset(cEps[:], 1.0e-5), inc=vcst)
        ONES = (vcst, vcst.n)

        WKR = R.dma("gpsimd", WKB[:, :], wk[:, :], R.sem("cwk"))
        WQR = R.dma("gpsimd", WQB[:, :], wq[:, :], R.sem("cwq"))
        WOR = R.dma("gpsimd", WOB[:, :], wout[:, :], R.sem("cwo"))

        H = dict(xf=xf, xo=xo, wq=WQB, wk=WKB, wv=wv, wout=WOB, w1=w1,
                 w3=w3, w2=w2, amask=amask, out=out, KS=KS, VS=VS, QS=QS,
                 XNT=XNT, XFT=XFT, AS=AS, W13S=W13S, W2S=W2S, cIdB=cIdB, cIdF=cIdF,
                 cH128=cH128, cH64=cH64, cOnesF=cOnesF, cEps=cEps, CONST=CONST,
                 ONES=ONES, WKR=WKR, WQR=WQR, WOR=WOR)

        R.fence([CONST, ONES, (cstf, cstf.n)])
        f1 = _phase12(nc, R, H)
        R.fence(f1)
        f3 = _phase3(nc, R, H)
        R.fence(f3)
        oSg = _phase4(nc, R, H)

        if debug:
            dsem = R.sem("dbgs")
            for name, (o, t) in dbg.items():
                R.dma("sync", o[:], t[:], dsem)
            R.op("sync", None, waits=((dsem, dsem.n),))

        R.op("sync", None, waits=tuple((s, s.n) for s in oSg))
        R.flush()

    return nc


def _phase12(nc, R, H):
    with ExitStack() as es:
        sX = es.enter_context(nc.sbuf_tensor("sX", [128, 2, D], F32)).ap()
        sStat = es.enter_context(nc.sbuf_tensor("sStat", [128, 2, 8], F32)).ap()
        sXn = es.enter_context(nc.sbuf_tensor("sXn", [128, 2, D], BF16)).ap()
        sXT = es.enter_context(nc.sbuf_tensor("sXT", [128, 2, NDC, 512], BF16)).ap()
        sW = es.enter_context(nc.sbuf_tensor("sW", [128, NDC, D], BF16)).ap()
        sVst = es.enter_context(nc.sbuf_tensor("sVst", [128, NH, 4, DH], BF16)).ap()
        sKev = es.enter_context(nc.sbuf_tensor("sKev", [128, 4, 512], BF16)).ap()
        sWld = es.enter_context(nc.sbuf_tensor("sWld", [128, 2, D], F32)).ap()
        sWldB = es.enter_context(nc.sbuf_tensor("sWldB", [128, 2, D], BF16)).ap()
        sWst = es.enter_context(nc.sbuf_tensor("sWst", [128, 2, D], BF16)).ap()
        sW2ldB = es.enter_context(nc.sbuf_tensor("sW2ldB", [128, 1, D], BF16)).ap()
        sStgA = es.enter_context(nc.sbuf_tensor("sStgA", [128, 16, 512], BF16)).ap()
        sW2ev = es.enter_context(nc.sbuf_tensor("sW2ev", [128, 4, 512], BF16)).ap()
        pT = [es.enter_context(nc.psum_tensor(f"pT{i}", [128, 512], BF16)).ap()
              for i in range(2)]
        pMM = [es.enter_context(nc.psum_tensor(f"pMM{i}", [128, 512], F32)).ap()
               for i in range(2)]
        pWT = [es.enter_context(nc.psum_tensor(f"pWT{i}", [128, 512], BF16)).ap()
               for i in range(2)]
        pSA = [es.enter_context(nc.psum_tensor(f"pSA{i}", [128, 512], F32)).ap()
               for i in range(2)]

        xldg = [R.sem("p1_xld0"), R.sem("p1_xld1")]
        xftg = [R.sem("p1_xft0"), R.sem("p1_xft1")]
        stg = [R.sem("p1_st0"), R.sem("p1_st1")]
        wres = R.sem("p1_wres")
        wldg = [R.sem("p1_wld0"), R.sem("p1_wld1")]
        vS = R.sem("p1_v")
        aS = R.sem("p1_a")
        tT = R.sem("p1_tT")
        tV = R.sem("p1_tV")
        tK = R.sem("p1_tK")
        tW = R.sem("p1_tW")
        tA = R.sem("p1_tA")
        kSg = [R.sem(f"p1_kS{i}") for i in range(4)]
        vsSg = [R.sem(f"p1_vs{i}") for i in range(4)]
        qSg = [R.sem(f"p1_qS{i}") for i in range(4)]
        wSg = [R.sem("p1_wS0"), R.sem("p1_wS1")]
        w2Sg = [R.sem(f"p1_w2S{i}") for i in range(4)]

        xf, xo, w1, w3, w2 = H["xf"], H["xo"], H["w1"], H["w3"], H["w2"]
        wq, wk, wv = H["wq"], H["wk"], H["wv"]
        KS, VS, QS, XNT, XFT = H["KS"], H["VS"], H["QS"], H["XNT"], H["XFT"]
        W13S, W2S = H["W13S"], H["W2S"]
        cIdB, cH128, cEps, CONST = H["cIdB"], H["cH128"], H["cEps"], H["CONST"]

        sX_free = [None, None]
        sXn_guard = [None, None]
        sXT_guard = [None, None]
        pT_free = [None, None]
        pMM_free = [None, None]
        pWT_free = [None, None]
        pSA_free = [None, None]
        sKev_free = [None] * 4
        sWld_free = [None, None]
        sWldB_free = [None, None]
        sWst_free = [None, None]
        sW2ldB_free = [None]
        sW2ev_free = [None] * 4
        sStgA_guard = [None]
        tmark = [[None] * 4, [None] * 4]
        xnt_mark = [None, None]
        state = {"prep": 0}
        pending = []

        def prep_head():
            pi = state["prep"]
            state["prep"] += 1
            if pi >= 128:
                return None
            mat_src = w1 if pi < 64 else w3
            off = 0 if pi < 64 else D
            u = pi % 64
            s = pi % 2
            R.dma("scalar", sWld[:, s, :],
                  mat_src.rearrange("(a rr) d -> a rr d", rr=128)[:, 2 * u:2 * u + 2, :],
                  wldg[s], waits=(sWld_free[s],))
            ld = (wldg[s], wldg[s].n)
            R.op("scalar",
                 lambda e, s=s: e.activation(out=sWldB[:, s, :], in_=sWld[:, s, :],
                                             func=AF.Identity),
                 waits=(ld, sWldB_free[s]), inc=aS)
            cast_done = (aS, aS.n)
            sWld_free[s] = cast_done
            return (pi, u, off, s, cast_done)

        def prep_tail(item):
            pi, u, off, s, cast_done = item
            for k in range(4):
                pb = (pi * 4 + k) % 2
                for q in range(4):
                    dc = 4 * k + q
                    R.op("tensor",
                         lambda e, s=s, dc=dc, pb=pb, q=q: e.transpose(
                             pWT[pb][:, 128 * q:128 * (q + 1)],
                             sWldB[:, s, 128 * dc:128 * (dc + 1)], cIdB[:]),
                         waits=(cast_done, CONST, pWT_free[pb]) if q == 0 else (),
                         inc=tW if q == 3 else None)
                tdone = (tW, tW.n)
                R.op("vector",
                     lambda e, k=k, s=s, pb=pb: e.tensor_copy(
                         out=sWst[:, s, 512 * k:512 * (k + 1)], in_=pWT[pb][:]),
                     waits=(tdone, sWst_free[s]), inc=vS)
                pWT_free[pb] = (vS, vS.n)
            evdone = (vS, vS.n)
            sWldB_free[s] = (tW, tW.n)
            for i in range(2):
                R.dma("sync", W13S[u, 64 * i:64 * (i + 1), off:off + D],
                      sWst[64 * i:64 * (i + 1), s, :], wSg[s],
                      waits=(evdone,) if i == 0 else ())
            sWst_free[s] = wSg[s]

        def drive_preps(nh):
            for item in pending:
                prep_tail(item)
            pending.clear()
            for _ in range(nh):
                item = prep_head()
                if item is not None:
                    pending.append(item)

        w2cd = {}

        def w2_load(wi, j):
            hh, dcb = wi // 4, wi % 4
            dc = 4 * dcb + j
            s = j % 2
            for i in range(4):
                R.dma("scalar", sWld[32 * i:32 * (i + 1), s, :],
                      w2[128 * dc + 32 * i:128 * dc + 32 * (i + 1),
                         2048 * hh:2048 * (hh + 1)],
                      wldg[s], waits=(sWld_free[s],) if i == 0 else ())
            w2cd[(wi, j, "ld")] = (wldg[s], wldg[s].n)

        def w2_castT(wi, j):
            s = j % 2
            R.op("scalar",
                 lambda e, s=s: e.activation(out=sW2ldB[:, 0, :],
                                             in_=sWld[:, s, :],
                                             func=AF.Identity),
                 waits=(w2cd[(wi, j, "ld")], sW2ldB_free[0]), inc=aS)
            cast_done = (aS, aS.n)
            sWld_free[s] = cast_done
            for k in range(4):
                pb = (wi * 16 + j * 4 + k) % 2
                for q in range(4):
                    a = 4 * k + q
                    w = ((cast_done, CONST, pWT_free[pb], sStgA_guard[0])
                         if q == 0 else ())
                    R.op("tensor",
                         lambda e, a=a, pb=pb, q=q: e.transpose(
                             pWT[pb][:, 128 * q:128 * (q + 1)],
                             sW2ldB[:, 0, 128 * a:128 * (a + 1)], cIdB[:]),
                         waits=w, inc=tW if q == 3 else None)
                tdone = (tW, tW.n)
                R.op("vector",
                     lambda e, k=k, j=j, pb=pb: e.tensor_copy(
                         out=sStgA[:, 4 * k:4 * (k + 1), 128 * j:128 * (j + 1)],
                         in_=pWT[pb][:].rearrange("p (a b) -> p a b", a=4)),
                     waits=(tdone,), inc=vS)
                pWT_free[pb] = (vS, vS.n)
            sW2ldB_free[0] = (tW, tW.n)
            w2cd[(wi, "stg", j)] = (vS, vS.n)

        def w2_stageA(wi):
            hh, dcb = wi // 4, wi % 4
            stg_ready = w2cd[(wi, "stg", 3)]
            for k in range(16):
                a = 16 * hh + k
                pb = k % 2
                R.op("tensor",
                     lambda e, k=k, pb=pb: e.matmul(
                         pSA[pb][:], cH128[:], sStgA[:, k, :],
                         start=True, stop=True),
                     waits=(stg_ready, CONST, pSA_free[pb]), inc=tA)
                mmdone = (tA, tA.n)
                eb = k % 4
                R.op("scalar",
                     lambda e, eb=eb, pb=pb: e.activation(
                         out=sW2ev[:, eb, :], in_=pSA[pb][:], func=AF.Identity,
                         scale=SCALE_Q),
                     waits=(mmdone, sW2ev_free[eb]), inc=aS)
                pSA_free[pb] = (aS, aS.n)
                ev = (aS, aS.n)
                for i in range(2):
                    R.dma("scalar",
                          W2S[32 * i:32 * (i + 1), 2 * a:2 * a + 2,
                              512 * dcb:512 * (dcb + 1)],
                          sW2ev[64 * i:64 * (i + 1), eb, :], w2Sg[eb],
                          waits=(ev,) if i == 0 else ())
                sW2ev_free[eb] = w2Sg[eb]
            sStgA_guard[0] = (tA, tA.n)

        w2_steps = []
        for wi in range(16):
            w2_steps.append(lambda wi=wi: w2_load(wi, 0))
            for j in range(4):
                if j < 3:
                    w2_steps.append(lambda wi=wi, j=j: (w2_load(wi, j + 1),
                                                        w2_castT(wi, j)))
                else:
                    w2_steps.append(lambda wi=wi: w2_castT(wi, 3))
            w2_steps.append(lambda wi=wi: w2_stageA(wi))
        w2_pos = [0]

        def drive_w2(n):
            for _ in range(n):
                if w2_pos[0] < len(w2_steps):
                    w2_steps[w2_pos[0]]()
                    w2_pos[0] += 1

        def ln_part(src_x, g, ts, tbase):
            t = tbase + 4 * g + ts
            b = t % 2
            for i in range(4):
                R.dma("sync", sX[32 * i:32 * (i + 1), b, :],
                      src_x[128 * (4 * g + ts) + 32 * i:
                            128 * (4 * g + ts) + 32 * (i + 1), :],
                      xldg[b], waits=(sX_free[b],) if i == 0 else ())
            xw = (xldg[b], xldg[b].n)
            xn_done = _ln_tile(R, vS, aS, sX[:, b, :], sXn[:, b, :],
                               sStat[:, b, :], sXn[:, b, :], xw, cEps,
                               sq_guard=(sXn_guard[b],))
            sX_free[b] = xn_done
            return (b, xn_done, t)

        def tp_part(part, gb):
            b, xn_done, t = part
            _xn_transposes(R, nc, sXn[:, b, :], sXT[:, gb, :, :], t % 4, pT,
                           pT_free, tT, vS, cIdB, xn_done, CONST,
                           (sXT_guard[gb],), t)
            sXn_guard[b] = (tT, tT.n)

        def wv_chunk(c):
            s = c % 2
            wvr = wv.rearrange("(c p) n -> p c n", p=128)
            for i in range(4):
                R.dma("scalar", sWld[32 * i:32 * (i + 1), s, :],
                      wvr[32 * i:32 * (i + 1), c, :],
                      wldg[s], waits=(sWld_free[s],) if i == 0 else ())
            ld = (wldg[s], wldg[s].n)
            R.op("scalar",
                 lambda e, c=c, s=s: e.activation(out=sW[:, c, :],
                                                  in_=sWld[:, s, :],
                                                  func=AF.Identity),
                 waits=(ld,), inc=aS)
            sWld_free[s] = (aS, aS.n)

        def load_resident(mat, ready_sem, pe_done):
            for c8 in range(8):
                R.dma("scalar", sW[:, 2 * c8:2 * (c8 + 1), :],
                      mat.rearrange("(c p) n -> p c n", p=128)[:, 2 * c8:2 * (c8 + 1), :],
                      wres, waits=(pe_done, ready_sem))
            return (wres, wres.n)

        for ts in range(4):
            part = ln_part(xf, 0, ts, 0)
            tp_part(part, 0)
            tmark[0][ts] = (vS, vS.n)
            for c in range(4 * ts, 4 * ts + 4):
                wv_chunk(c)
        wv_rdy = (aS, aS.n)
        A_pe_g6 = None
        for g in range(8):
            gb = g % 2
            R.dma("sync", XFT[g, :, :, :], sXT[:, gb, :, :], stg[gb],
                  waits=(tmark[gb][3],))
            for vc in range(4):
                part = ln_part(xf, g + 1, vc, 0) if g < 7 else None
                for ts4 in range(4):
                    pb = (vc * 4 + ts4) % 2
                    for dc in range(NDC):
                        R.op("tensor",
                             lambda e, gb=gb, vc=vc, ts4=ts4, dc=dc, pb=pb:
                             e.matmul(
                                 pMM[pb][:],
                                 sXT[:, gb, dc, 128 * ts4:128 * (ts4 + 1)],
                                 sW[:, dc, 512 * vc:512 * (vc + 1)],
                                 start=(dc == 0), stop=(dc == NDC - 1)),
                             waits=(tmark[gb][ts4], wv_rdy, pMM_free[pb])
                             if dc == 0 else (),
                             inc=tV if dc == NDC - 1 else None)
                    mmdone = (tV, tV.n)
                    R.op("vector",
                         lambda e, vc=vc, ts4=ts4, pb=pb: e.tensor_copy(
                             out=sVst[:, 4 * vc:4 * (vc + 1), ts4, :],
                             in_=pMM[pb][:].rearrange("p (h r) -> p h r", h=4)),
                         waits=(mmdone, vsSg[vc]), inc=vS)
                    pMM_free[pb] = (vS, vS.n)
                vgrp_done = (vS, vS.n)
                if part is not None:
                    tp_part(part, (g + 1) % 2)
                    tmark[(g + 1) % 2][vc] = (vS, vS.n)
                drive_preps(2)
                for hq in range(4):
                    h = 4 * vc + hq
                    R.dma("scalar", VS[h, :, 4 * g:4 * (g + 1), :],
                          sVst[:, h, :, :], vsSg[vc], waits=(vgrp_done,))
            sXT_guard[gb] = stg[gb]
            if g == 6:
                A_pe_g6 = (tV, tV.n)

        A_pe = (tV, tV.n)
        xft_wait = [None, None]
        xft_wait[0] = R.dma("sync", sXT[:, 0, :, :], XFT[0, :, :, :], xftg[0],
                            waits=(A_pe_g6, stg[0]))
        wk_rdy = load_resident(wk, H["WKR"], A_pe)
        sXT_B_free = [A_pe, A_pe]
        for g in range(8):
            b = g % 2
            if g + 1 < 8:
                nb = (g + 1) % 2
                xft_wait[nb] = R.dma("sync", sXT[:, nb, :, :],
                                     XFT[g + 1, :, :, :], xftg[nb],
                                     waits=(A_pe, stg[nb], sXT_B_free[nb]))
            for kc in range(NDC):
                pb = kc % 2
                for dc in range(NDC):
                    R.op("tensor",
                         lambda e, b=b, kc=kc, dc=dc, pb=pb: e.matmul(
                             pMM[pb][:], sW[:, dc, 128 * kc:128 * (kc + 1)],
                             sXT[:, b, dc, :],
                             start=(dc == 0), stop=(dc == NDC - 1)),
                         waits=(xft_wait[b], wk_rdy, pMM_free[pb])
                         if dc == 0 else (),
                         inc=tK if dc == NDC - 1 else None)
                mmdone = (tK, tK.n)
                eb = kc % 4
                R.op("scalar",
                     lambda e, eb=eb, pb=pb: e.activation(
                         out=sKev[:, eb, :], in_=pMM[pb][:], func=AF.Identity),
                     waits=(mmdone, sKev_free[eb]), inc=aS)
                pMM_free[pb] = (aS, aS.n)
                ev = (aS, aS.n)
                for i in range(2):
                    R.dma("sync",
                          KS[128 * kc + 64 * i:128 * kc + 64 * (i + 1),
                             512 * g:512 * (g + 1)],
                          sKev[64 * i:64 * (i + 1), eb, :], kSg[eb],
                          waits=(ev,) if i == 0 else ())
                sKev_free[eb] = kSg[eb]
                if kc % 2 == 1:
                    drive_preps(1)
            sXT_B_free[b] = (tK, tK.n)
        drive_preps(0)

        B_pe = (tK, tK.n)
        sXT_guard[0] = sXT_guard[1] = None
        for ts in range(4):
            part = ln_part(xo, 0, ts, 32)
            tp_part(part, 0)
        xnt_mark[0] = (vS, vS.n)
        wq_rdy = load_resident(wq, H["WQR"], B_pe)
        for g in range(4):
            gb = g % 2
            R.dma("sync", XNT[g, :, :, :], sXT[:, gb, :, :], stg[gb],
                  waits=(xnt_mark[gb],))
            for qc in range(NDC):
                part = (ln_part(xo, g + 1, qc - 1, 32)
                        if (1 <= qc <= 4 and g < 3) else None)
                pb = qc % 2
                for dc in range(NDC):
                    R.op("tensor",
                         lambda e, gb=gb, qc=qc, dc=dc, pb=pb: e.matmul(
                             pMM[pb][:], sW[:, dc, 128 * qc:128 * (qc + 1)],
                             sXT[:, gb, dc, :],
                             start=(dc == 0), stop=(dc == NDC - 1)),
                         waits=(xnt_mark[gb], wq_rdy, pMM_free[pb])
                         if dc == 0 else (),
                         inc=tK if dc == NDC - 1 else None)
                mmdone = (tK, tK.n)
                eb = qc % 4
                R.op("scalar",
                     lambda e, eb=eb, pb=pb: e.activation(
                         out=sKev[:, eb, :], in_=pMM[pb][:], func=AF.Identity,
                         scale=SCALE_Q),
                     waits=(mmdone, sKev_free[eb]), inc=aS)
                pMM_free[pb] = (aS, aS.n)
                ev = (aS, aS.n)
                for i in range(2):
                    R.dma("sync", QS[g, 64 * i:64 * (i + 1), qc, :],
                          sKev[64 * i:64 * (i + 1), eb, :], qSg[eb],
                          waits=(ev,) if i == 0 else ())
                sKev_free[eb] = qSg[eb]
                if part is not None:
                    tp_part(part, (g + 1) % 2)
                    if qc == 4:
                        xnt_mark[(g + 1) % 2] = (vS, vS.n)
                drive_w2(2 if qc % 2 == 0 else 1)
            sXT_guard[gb] = stg[gb]

        while state["prep"] < 128 or pending:
            drive_preps(1)
        drive_w2(len(w2_steps))

        R.flush()
        sems = (xldg + xftg + stg + kSg + qSg + w2Sg + wldg + wSg + vsSg +
                [wres, vS, aS, tT, tV, tK, tW, tA])
        return [(s, s.n) for s in sems]


def _phase3(nc, R, H):
    with ExitStack() as es:
        sAmL = es.enter_context(nc.sbuf_tensor("sAmL", [128, 4, 4, 1024], BF16)).ap()
        sS = es.enter_context(nc.sbuf_tensor("sS", [128, 4, 4096], F32)).ap()
        sP = es.enter_context(nc.sbuf_tensor("sP", [128, 2, 4, 4096], BF16)).ap()
        sPT = es.enter_context(nc.sbuf_tensor("sPT", [128, 4, 512], BF16)).ap()
        sK = es.enter_context(nc.sbuf_tensor("sK", [128, 2, 4096], BF16)).ap()
        sV = es.enter_context(nc.sbuf_tensor("sV", [128, 2, 32, DH], BF16)).ap()
        sQc = es.enter_context(nc.sbuf_tensor("sQc", [128, 2, 512], BF16)).ap()
        sCM = es.enter_context(nc.sbuf_tensor("sCM", [128, 4, 4], F32)).ap()
        sNM = es.enter_context(nc.sbuf_tensor("sNM", [128, 2, 4], F32)).ap()
        sL = es.enter_context(nc.sbuf_tensor("sL", [128, 2, 4], F32)).ap()
        sRi = es.enter_context(nc.sbuf_tensor("sRi", [128, 2, 4], F32)).ap()
        sA = es.enter_context(nc.sbuf_tensor("sA", [128, 2, 512], BF16)).ap()
        pS = [es.enter_context(nc.psum_tensor(f"p3pS{i}", [128, 2, 512], F32)).ap()
              for i in range(2)]
        pTT = es.enter_context(nc.psum_tensor("p3pTT", [128, 2, 512], BF16)).ap()
        pO = [es.enter_context(nc.psum_tensor(f"p3pO{i}", [128, 512], F32)).ap()
              for i in range(2)]

        dLg = [R.sem("p3_dL0"), R.sem("p3_dL1")]
        mld = R.sem("p3_mld")
        dAg = [R.sem("p3_dA0"), R.sem("p3_dA1")]
        tQK = R.sem("p3_tQK")
        tTP = R.sem("p3_tTP")
        tPV = R.sem("p3_tPV")
        aSE = R.sem("p3_aSE")
        aPT = R.sem("p3_aPT")
        aE = R.sem("p3_aE")
        aA = R.sem("p3_aA")
        vSE = R.sem("p3_vSE")
        vSum = R.sem("p3_vSum")
        gP = R.sem("p3_gP")
        gM = R.sem("p3_gM")

        KS, VS, QS, AS, amask = H["KS"], H["VS"], H["QS"], H["AS"], H["amask"]
        CONST = H["CONST"]

        R.dma("sync", sAmL[:, :, :, :],
              amask.rearrange("l i p w -> p l i w"), mld)
        MASKS = (mld, mld.n)
        ld_free = [None, None]
        pS_free = [None, None]
        pTT_free = [None]
        pO_free = [None, None]
        sA_free = [None]
        sPT_base = [0]
        sNM_guard = [None, None]
        sL_guard = [None, None]
        sRi_guard = [None, None]
        state = {}
        units = [(l, h) for h in range(NH) for l in range(4)]

        def deferred(p):
            l, h, kv, nb, n5, par, _exp_done, scale_done, _tTf = state[p]
            ev_jb = {}
            pv_base = [None]

            def emit_Tpair(jp):
                for sub in range(2):
                    jb = 2 * jp + sub
                    for i2 in range(4):
                        R.op("tensor",
                             lambda e, jb=jb, i2=i2, sub=sub, par=par: e.transpose(
                                 pTT[:, sub, 128 * i2:128 * (i2 + 1)],
                                 sP[:, par, i2, 128 * jb:128 * (jb + 1)],
                                 H["cIdB"][:]),
                             waits=(scale_done, CONST, pTT_free[0])
                             if (sub == 0 and i2 == 0) else (),
                             inc=tTP if (sub == 1 and i2 == 3) else None)
                tdone = (tTP, tTP.n)
                slot = (2 * jp) % 4
                w = [tdone]
                if pv_base[0] is None:
                    pv_base[0] = tPV.n
                if jp >= 2:
                    w.append((tPV, pv_base[0] + 2 * jp - 2))
                if jp % 2 == 0:
                    R.op("vector",
                         lambda e, jp=jp, slot=slot: e.tensor_copy(
                             out=sPT[:, slot:slot + 2, :], in_=pTT[:, :, :]),
                         waits=tuple(w), inc=vSE)
                    ev_jb[jp] = (vSE, vSE.n)
                else:
                    R.op("scalar",
                         lambda e, jp=jp, slot=slot: e.activation(
                             out=sPT[:, slot:slot + 2, :], in_=pTT[:, :, :],
                             func=AF.Identity),
                         waits=tuple(w), inc=aPT)
                    ev_jb[jp] = (aPT, aPT.n)
                pTT_free[0] = ev_jb[jp]

            def emit_PV(jb):
                w = [ev_jb[jb // 2]]
                if jb == 0:
                    w.append(pO_free[par])
                R.op("tensor",
                     lambda e, par=par, jb=jb, nb=nb: e.matmul(
                         pO[par][:], sV[:, par, jb, :], sPT[:, jb % 4, :],
                         start=(jb == 0), stop=(jb == nb - 1)),
                     waits=tuple(w), inc=tPV)

            steps = [lambda: emit_Tpair(0)]
            for jp in range(1, nb // 2):
                steps.append(lambda jp=jp: (emit_Tpair(jp), emit_PV(2 * jp - 2),
                                            emit_PV(2 * jp - 1)))
            steps.append(lambda: (emit_PV(nb - 2), emit_PV(nb - 1)))

            def tail():
                pv_done = (tPV, tPV.n)
                ld_free[par] = pv_done
                R.op("scalar",
                     lambda e, par=par: e.activation(
                         out=sA[:, 0, :], in_=pO[par][:], func=AF.Identity),
                     waits=(pv_done, sA_free[0]), inc=aA)
                pO_free[par] = (aA, aA.n)
                R.dma("scalar", AS[h, :, 512 * l:512 * (l + 1)], sA[:, 0, :],
                      dAg[par], waits=((aA, aA.n),))
                sA_free[0] = dAg[par]
            return steps, tail

        loads_mark = {}

        def emit_loads(j):
            if j >= len(units):
                return
            lj, hj = units[j]
            kvj = KVS[lj]
            nbj = kvj // 128
            pj = j % 2
            R.dma("sync", sK[:, pj, 0:kvj],
                  KS[128 * hj:128 * (hj + 1), 0:kvj],
                  dLg[pj], waits=(ld_free[pj],))
            R.dma("sync", sV[:, pj, 0:nbj, :], VS[hj, :, 0:nbj, :], dLg[pj])
            R.dma("sync", sQc[:, pj, :], QS[lj, :, hj, :], dLg[pj])
            loads_mark[j] = (dLg[pj], dLg[pj].n)

        emit_loads(0)
        for idx, (l, h) in enumerate(units):
            kv = KVS[l]
            nb = kv // 128
            n5 = kv // 512
            par = idx % 2
            mask_ready = MASKS
            lds = loads_mark[idx]

            prev_exp_rows = state[idx - 1][6] if idx > 0 else [None] * 4
            pexp_guard = state[idx - 1][8] if idx > 1 else None
            row_started = [False] * 4
            exp_rows = [None] * 4
            bcnt = [0]
            nbat = n5 // 2

            def emit_qk_batch(i2, bi, idx=idx, l=l, kv=kv, n5=n5, nbat=nbat,
                              par=par, lds=lds, mask_ready=mask_ready,
                              prev_exp_rows=prev_exp_rows,
                              pexp_guard=pexp_guard):
                pp = bcnt[0] % 2
                for k in range(2):
                    j5 = 2 * bi + k
                    R.op("tensor",
                         lambda e, par=par, i2=i2, j5=j5, k=k, pp=pp: e.matmul(
                             pS[pp][:, k, :],
                             sQc[:, par, 128 * i2:128 * (i2 + 1)],
                             sK[:, par, 512 * j5:512 * (j5 + 1)],
                             start=True, stop=True),
                         waits=(lds, pS_free[pp]) if k == 0 else (),
                         inc=tQK if k == 1 else None)
                mmdone = (tQK, tQK.n)
                j0 = 2 * bi
                w = [mmdone]
                if not row_started[i2]:
                    w.append(prev_exp_rows[i2])
                    row_started[i2] = True
                if bi == nbat - 1:
                    w.append(mask_ready)
                    R.op("vector",
                         lambda e, i2=i2, j0=j0, pp=pp, l=l: e.tensor_tensor(
                             out=sS[:, i2, 512 * j0:512 * (j0 + 2)]
                             .rearrange("p (a b) -> p a b", a=2),
                             in0=pS[pp][:, :, :],
                             in1=sAmL[:, l, i2, :]
                             .rearrange("p (a b) -> p a b", a=2),
                             op=ALU.add),
                         waits=tuple(w), inc=vSE)
                    ev = (vSE, vSE.n)
                elif bcnt[0] % 4 == 3:
                    R.op("vector",
                         lambda e, i2=i2, j0=j0, pp=pp: e.tensor_copy(
                             out=sS[:, i2, 512 * j0:512 * (j0 + 2)],
                             in_=pS[pp][:, :, :]),
                         waits=tuple(w), inc=vSE)
                    ev = (vSE, vSE.n)
                else:
                    R.op("scalar",
                         lambda e, i2=i2, j0=j0, pp=pp: e.activation(
                             out=sS[:, i2, 512 * j0:512 * (j0 + 2)],
                             in_=pS[pp][:, :, :], func=AF.Identity),
                         waits=tuple(w), inc=aSE)
                    ev = (aSE, aSE.n)
                pS_free[pp] = ev
                R.op("vector",
                     lambda e, i2=i2, bi=bi, j0=j0: e.reduce_max(
                         sCM[:, i2, bi:bi + 1],
                         sS[:, i2, 512 * j0:512 * (j0 + 2)], axis=AX.X),
                     waits=(ev,), inc=vSE)
                bcnt[0] += 1
                if bi == nbat - 1:
                    R.op("vector",
                         lambda e, i2=i2, nbat=nbat, par=par: e.reduce_max(
                             sNM[:, par, i2:i2 + 1], sCM[:, i2, 0:nbat],
                             axis=AX.X, negate=True),
                         waits=((vSE, vSE.n), sNM_guard[par]), inc=vSE)
                    nm = (vSE, vSE.n)
                    R.op("scalar",
                         lambda e, i2=i2, kv=kv, par=par: e.activation(
                             out=sP[:, par, i2, 0:kv], in_=sS[:, i2, 0:kv],
                             func=AF.Exp, bias=sNM[:, par, i2:i2 + 1],
                             accum_out=sL[:, par, i2:i2 + 1]),
                         waits=(nm, sL_guard[par], pexp_guard), inc=aE)
                    exp_rows[i2] = (aE, aE.n)
                    R.op("vector",
                         lambda e, i2=i2, par=par: e.reciprocal(
                             sRi[:, par, i2:i2 + 1], sL[:, par, i2:i2 + 1]),
                         waits=((aE, aE.n), sRi_guard[par]), inc=vSum)
                    R.op("vector",
                         lambda e, i2=i2, kv=kv, par=par: e.tensor_scalar_mul(
                             sP[:, par, i2, 0:kv], sP[:, par, i2, 0:kv],
                             sRi[:, par, i2:i2 + 1]),
                         waits=((vSum, vSum.n),), inc=gP)

            qk_steps = [(i2, bi) for i2 in range(4) for bi in range(nbat)]
            if idx > 0:
                pair_steps, def_tail = deferred(idx - 1)
            else:
                pair_steps, def_tail = [], None
            nq, nd = len(qk_steps), len(pair_steps)
            qi = di = 0
            while qi < nq or di < nd:
                take_q = 2 if qi == 0 else 1
                for _ in range(take_q):
                    if qi < nq:
                        emit_qk_batch(*qk_steps[qi])
                        qi += 1
                if nq - qi <= 0:
                    while di < nd:
                        pair_steps[di]()
                        di += 1
                else:
                    want = int(round((qi / nq) * nd))
                    while di < min(want, nd):
                        pair_steps[di]()
                        di += 1
            if def_tail is not None:
                def_tail()
            emit_loads(idx + 1)

            sNM_guard[par] = (aE, aE.n)
            sL_guard[par] = (vSum, vSum.n)
            sRi_guard[par] = (gP, gP.n)
            state[idx] = (l, h, kv, nb, n5, par, exp_rows, (gP, gP.n),
                          (tTP, tTP.n))
        last_steps, last_tail = deferred(len(units) - 1)
        for st_fn in last_steps:
            st_fn()
        last_tail()

        R.flush()
        sems = dLg + dAg + [mld, tQK, tTP, tPV, aSE, aPT, aE, aA, vSE,
                            vSum, gP]
        return [(s, s.n) for s in sems]


def _phase4(nc, R, H):
    with ExitStack() as es:
        sH2 = es.enter_context(nc.sbuf_tensor("sH2", [128, NU, 512], BF16)).ap()
        sXnTc = es.enter_context(nc.sbuf_tensor("sXnTc", [128, NDC, 512], BF16)).ap()
        sW13t = es.enter_context(nc.sbuf_tensor("sW13t", [128, 3, 2 * D], BF16)).ap()
        sW2t = es.enter_context(nc.sbuf_tensor("sW2t", [128, 4, 1024], BF16)).ap()
        sWoT = es.enter_context(nc.sbuf_tensor("sWoT", [128, 2, 1024], BF16)).ap()
        sAt = es.enter_context(nc.sbuf_tensor("sAt", [128, 2, 512], BF16)).ap()
        sSil = es.enter_context(nc.sbuf_tensor("sSil", [128, 2, 512], F32)).ap()
        sHb = es.enter_context(nc.sbuf_tensor("sHb", [128, 2, 512], BF16)).ap()
        sFM = es.enter_context(nc.sbuf_tensor("sFM", [128, NDC, 512], F32)).ap()
        sXo = es.enter_context(nc.sbuf_tensor("p4sXo", [128, 2, D], F32)).ap()
        sOutT = es.enter_context(nc.sbuf_tensor("sOutT", [128, 2, D], F32)).ap()

        xnl = R.sem("p4_xnl")
        wlg = [R.sem("p4_wl0"), R.sem("p4_wl1"), R.sem("p4_wl2")]
        w2tg = [R.sem(f"p4_w2t{i}") for i in range(4)]
        atg = [R.sem("p4_at0"), R.sem("p4_at1")]
        glg = [R.sem("p4_gl0"), R.sem("p4_gl1")]
        xlg = [R.sem("p4_xl0"), R.sem("p4_xl1")]
        oSg = [R.sem("p4_oS0"), R.sem("p4_oS1")]
        tU = R.sem("p4_tU")
        tL = R.sem("p4_tL")
        tBB = R.sem("p4_tB")
        tG = R.sem("p4_tG")
        tOT = R.sem("p4_tOT")
        aSi = R.sem("p4_aSi")
        aH2 = R.sem("p4_aH2")
        aG = R.sem("p4_aG")
        vMu = R.sem("p4_vMu")
        vO = R.sem("p4_vO")

        xo, wout = H["xo"], H["wout"]
        XNT, AS, W13S, W2S = H["XNT"], H["AS"], H["W13S"], H["W2S"]
        cH64, cIdF, out, CONST = H["cH64"], H["cIdF"], H["out"], H["CONST"]

        sW13_free = [None, None, None]
        sW2_free = [None, None, None, None]
        sWo_free = [None, None]
        sAt_free = [None, None]
        sSil_free = [None, None]
        sHb_free = [None, None]
        sXo_free = [None, None]
        sOutT_free = [None, None]
        sH2_guard = [None]
        sXnTc_guard = [None]
        sFM_guard = [None]
        bank_guard = [None]

        for c in range(4):
            R.dma("sync", sXnTc[:, :, :], XNT[c, :, :, :], xnl,
                  waits=(sXnTc_guard[0],))
            xnt_ready = (xnl, xnl.n)

            with ExitStack() as es_b:
                pUp = [es_b.enter_context(
                    nc.psum_tensor(f"p4pUp{c}{i}", [128, 512], F32)).ap()
                    for i in range(2)]
                pLin = [es_b.enter_context(
                    nc.psum_tensor(f"p4pLin{c}{i}", [128, 512], F32)).ap()
                    for i in range(2)]
                pB = [es_b.enter_context(
                    nc.psum_tensor(f"p4pB{c}{i}", [128, 512], F32)).ap()
                    for i in range(2)]
                pUp_free = [bank_guard[0], bank_guard[0]]
                pLin_free = [bank_guard[0], bank_guard[0]]
                pB_free = [bank_guard[0], bank_guard[0]]
                mul_done = {}

                def stage_b(u):
                    b = u % 2
                    R.op("tensor",
                         lambda e, b=b: e.matmul(
                             pB[b][:], cH64[:], sHb[:, b, :],
                             start=True, stop=True),
                         waits=(mul_done[u], CONST, pB_free[b], sH2_guard[0]),
                         inc=tBB)
                    sHb_free[b] = (tBB, tBB.n)
                    R.op("scalar",
                         lambda e, b=b, u=u: e.activation(
                             out=sH2[:, u, :], in_=pB[b][:], func=AF.Copy,
                             scale=0.125),
                         waits=((tBB, tBB.n),), inc=aH2)
                    pB_free[b] = (aH2, aH2.n)

                for u in range(NU):
                    b = u % 2
                    b3 = u % 3
                    R.dma("sync", sW13t[:, b3, :], W13S[u, :, :], wlg[b3],
                          waits=(sW13_free[b3],))
                    wld = (wlg[b3], wlg[b3].n)
                    for dc in range(NDC):
                        R.op("tensor",
                             lambda e, b=b, b3=b3, dc=dc: e.matmul(
                                 pUp[b][:], sW13t[:, b3, 128 * dc:128 * (dc + 1)],
                                 sXnTc[:, dc, :], start=(dc == 0),
                                 stop=(dc == NDC - 1)),
                             waits=(wld, xnt_ready, pUp_free[b]) if dc == 0 else (),
                             inc=tU if dc == NDC - 1 else None)
                    for dc in range(NDC):
                        R.op("tensor",
                             lambda e, b=b, b3=b3, dc=dc: e.matmul(
                                 pLin[b][:], sW13t[:, b3, D + 128 * dc:D + 128 * (dc + 1)],
                                 sXnTc[:, dc, :], start=(dc == 0),
                                 stop=(dc == NDC - 1)),
                             waits=(pLin_free[b],) if dc == 0 else (),
                             inc=tL if dc == NDC - 1 else None)
                    sW13_free[b3] = (tL, tL.n)
                    if u > 0:
                        stage_b(u - 1)
                    R.op("scalar",
                         lambda e, b=b: e.activation(
                             out=sSil[:, b, :], in_=pUp[b][:], func=AF.Silu),
                         waits=((tU, tU.n), sSil_free[b]), inc=aSi)
                    pUp_free[b] = (aSi, aSi.n)
                    R.op("vector",
                         lambda e, b=b: e.tensor_tensor(
                             out=sHb[:, b, :], in0=pLin[b][:],
                             in1=sSil[:, b, :], op=ALU.mult),
                         waits=((tL, tL.n), (aSi, aSi.n), sHb_free[b]), inc=vMu)
                    pLin_free[b] = (vMu, vMu.n)
                    sSil_free[b] = (vMu, vMu.n)
                    mul_done[u] = (vMu, vMu.n)
                stage_b(NU - 1)
                h2_ready = (aH2, aH2.n)
                sXnTc_guard[0] = (tL, tL.n)
                R.flush()

            with ExitStack() as es_c:
                pG = es_c.enter_context(
                    nc.psum_tensor(f"p4pG{c}", [128, 8, 512], F32)).ap()
                pG_free = (aH2, aH2.n)
                for half in range(2):
                    for u in range(NU):
                        b = (half * NU + u) % 4
                        R.dma("scalar", sW2t[:, b, :],
                              W2S[u, :, 1024 * half:1024 * (half + 1)], w2tg[b],
                              waits=(sW2_free[b],))
                        wld = (w2tg[b], w2tg[b].n)
                        for d8 in range(8):
                            w = ()
                            if u == 0:
                                w = (wld, h2_ready, sFM_guard[0], pG_free)
                            elif d8 == 0:
                                w = (wld,)
                            R.op("tensor",
                                 lambda e, b=b, d8=d8, u=u: e.matmul(
                                     pG[:, d8, :],
                                     sW2t[:, b, 128 * d8:128 * (d8 + 1)],
                                     sH2[:, u, :], start=(u == 0), stop=False),
                                 waits=w, inc=tG if d8 == 7 else None)
                        sW2_free[b] = (tG, tG.n)
                    for cc in range(NDC):
                        b = cc % 2
                        R.dma("scalar", sWoT[:, b, :],
                              wout[128 * cc:128 * (cc + 1),
                                   1024 * half:1024 * (half + 1)],
                              glg[b], waits=(sWo_free[b], H["WOR"]))
                        R.dma("scalar", sAt[:, b, :],
                              AS[cc, :, 512 * c:512 * (c + 1)], atg[b],
                              waits=(sAt_free[b],))
                        wld2 = ((glg[b], glg[b].n), (atg[b], atg[b].n))
                        for d8 in range(8):
                            R.op("tensor",
                                 lambda e, b=b, d8=d8, cc=cc: e.matmul(
                                     pG[:, d8, :],
                                     sWoT[:, b, 128 * d8:128 * (d8 + 1)],
                                     sAt[:, b, :], start=False,
                                     stop=(cc == NDC - 1)),
                                 waits=wld2 if d8 == 0 else (),
                                 inc=tG if d8 == 7 else None)
                        sWo_free[b] = (tG, tG.n)
                        sAt_free[b] = (tG, tG.n)
                    grp_done = (tG, tG.n)
                    R.op("scalar",
                         lambda e, half=half: e.activation(
                             out=sFM[:, 8 * half:8 * (half + 1), :],
                             in_=pG[:, :, :], func=AF.Copy),
                         waits=(grp_done,), inc=aG)
                    pG_free = (aG, aG.n)
                sH2_guard[0] = (tG, tG.n)
                fm_ready = (aG, aG.n)
                R.flush()

            with ExitStack() as es_d:
                pOq = es_d.enter_context(
                    nc.psum_tensor(f"p4pOq{c}", [128, 4, 512], F32)).ap()
                pOq_free = [(aG, aG.n)] * 4
                for ts in range(4):
                    b = ts % 2
                    t = 4 * c + ts
                    R.dma("sync", sXo[:, b, :], xo[128 * t:128 * (t + 1), :],
                          xlg[b], waits=(sXo_free[b],))
                    xw = (xlg[b], xlg[b].n)
                    for dc in range(NDC):
                        qb = dc // 4
                        R.op("tensor",
                             lambda e, dc=dc, ts=ts, qb=qb: e.transpose(
                                 pOq[:, qb, 128 * (dc % 4):128 * (dc % 4 + 1)],
                                 sFM[:, dc, 128 * ts:128 * (ts + 1)], cIdF[:]),
                             waits=(fm_ready, CONST, pOq_free[qb])
                             if dc % 4 == 0 else (),
                             inc=tOT if dc % 4 == 3 else None)
                    tdone = (tOT, tOT.n)
                    for qb in range(4):
                        R.op("vector",
                             lambda e, qb=qb, b=b: e.tensor_tensor(
                                 out=sOutT[:, b, 512 * qb:512 * (qb + 1)],
                                 in0=pOq[:, qb, :],
                                 in1=sXo[:, b, 512 * qb:512 * (qb + 1)],
                                 op=ALU.add),
                             waits=(tdone, xw, sOutT_free[b]), inc=vO)
                        pOq_free[qb] = (vO, vO.n)
                    sXo_free[b] = (vO, vO.n)
                    R.dma("sync", out[128 * t:128 * (t + 1), :], sOutT[:, b, :],
                          oSg[b], waits=((vO, vO.n),))
                    sOutT_free[b] = oSg[b]
                sFM_guard[0] = (tOT, tOT.n)
                bank_guard[0] = (vO, vO.n)
                R.flush()

        return oSg


def kernel(L0_attn_x, L0_attn_w_qkv, L0_attn_w_out, L0_ffn_W1, L0_ffn_W2,
           L0_ffn_W3):
    x = np.asarray(L0_attn_x, dtype=np.float32)
    wqkv = np.asarray(L0_attn_w_qkv, dtype=np.float32)
    wq = np.ascontiguousarray(wqkv[:, 0:D])
    wk = np.ascontiguousarray(wqkv[:, D:2 * D])
    wv = np.ascontiguousarray(wqkv[:, 2 * D:3 * D])
    wout = np.ascontiguousarray(np.asarray(L0_attn_w_out, dtype=np.float32))
    w1 = np.ascontiguousarray(np.asarray(L0_ffn_W1, dtype=np.float32))
    w2 = np.ascontiguousarray(np.asarray(L0_ffn_W2, dtype=np.float32))
    w3 = np.ascontiguousarray(np.asarray(L0_ffn_W3, dtype=np.float32))

    h128c = _hadamard(128)
    h64c = _h64_i2()
    identf = np.eye(128, dtype=np.float32)
    masks = [np.ascontiguousarray(_amask(j)) for j in (0, 1)]

    nc = build_program()
    in_maps = []
    for c in range(8):
        b, j = c // 2, c % 2
        xb = np.ascontiguousarray(x[b])
        xov = np.ascontiguousarray(
            x[b].reshape(8, 512, D)[S_SETS[j]].reshape(OWN, D))
        in_maps.append(dict(xf=xb, xo=xov, wq=wq, wk=wk, wv=wv, wout=wout,
                            w1=w1, w3=w3, w2=w2, amask=masks[j], h128=h128c,
                            h64=h64c, identf=identf))
    res = run_bass_kernel_spmd(nc, in_maps, core_ids=list(range(8)))
    outa = np.empty((4, SEQ, D), dtype=np.float32)
    for c in range(8):
        b, j = c // 2, c % 2
        oc = res.results[c]["out"].reshape(4, 512, D)
        for l, g in enumerate(S_SETS[j]):
            outa[b, 512 * g:512 * (g + 1)] = oc[l]
    return outa


# version 24
# speedup vs baseline: 1.1223x; 1.0367x over previous
import numpy as np
import ml_dtypes
import concourse.bass as bass
import concourse.mybir as mybir
from concourse.bass_utils import run_bass_kernel_spmd
from contextlib import ExitStack

F32 = mybir.dt.float32
BF16 = mybir.dt.bfloat16
AF = mybir.ActivationFunctionType
ALU = mybir.AluOpType
AX = mybir.AxisListType

D = 2048
SEQ = 4096
OWN = 2048
NH = 16
DH = 128
FH = 8192
NDC = 16
NU = 64
SCALE_Q = 1.0 / float(np.sqrt(128.0))
S_SETS = ([0, 3, 4, 7], [1, 2, 5, 6])
KVS = [1024, 2048, 3072, 4096]
NEG = -1.0e30

ENGINES = ("tensor", "vector", "scalar", "gpsimd", "sync")


class Sem:
    __slots__ = ("h", "n", "name")

    def __init__(self, h, name):
        self.h = h
        self.n = 0
        self.name = name


class Rec:

    def __init__(self, nc, blk):
        self.nc = nc
        self.blk = blk
        self.ops = {e: [] for e in ENGINES}
        self.wm = {e: {} for e in ENGINES}

    def sem(self, name):
        return Sem(self.nc.alloc_semaphore(name), name)

    def op(self, eng, fn, waits=(), inc=None, inc_by=1):
        lst = self.ops[eng]
        for w in waits:
            if w is None:
                continue
            if isinstance(w, Sem):
                s, v = w, w.n
            else:
                s, v = w
            if v is None or v <= 0:
                continue
            if self.wm[eng].get(id(s), -1) < v:
                lst.append(("w", s, v))
                self.wm[eng][id(s)] = v
        if inc is not None:
            inc.n += inc_by
        if fn is not None:
            lst.append(("x", fn, inc, inc_by))
        return (inc, inc.n) if inc is not None else None

    def dma(self, q, out, in_, sem, waits=()):
        self.op(q, (lambda e, o=out, i=in_: e.dma_start(out=o, in_=i)),
                waits, inc=sem, inc_by=16)
        return (sem, sem.n)

    def dmaT(self, q, out, in_, sem, waits=()):
        self.op(q, (lambda e, o=out, i=in_: e.dma_start(out=o, in_=i,
                                                        transpose=True)),
                waits, inc=sem, inc_by=16)
        return (sem, sem.n)

    def fence(self, fences):
        for e in ENGINES:
            for (s, v) in fences:
                if v > 0 and self.wm[e].get(id(s), -1) < v:
                    self.ops[e].append(("w", s, v))
                    self.wm[e][id(s)] = v

    def flush(self):
        for e in ENGINES:
            lst = self.ops[e]
            if not lst:
                continue

            def body(eng, lst=tuple(lst)):
                for item in lst:
                    if item[0] == "w":
                        eng.wait_ge(item[1].h, item[2])
                    else:
                        _, fn, inc, k = item
                        ins = fn(eng)
                        if inc is not None:
                            ins.then_inc(inc.h, k)
            getattr(self.blk, e)(body)
            self.ops[e] = []


def _popcount_and(n):
    i = np.arange(n)
    pc = np.zeros((n, n), dtype=np.int64)
    a = i[:, None] & i[None, :]
    while np.any(a):
        pc += a & 1
        a >>= 1
    return pc


def _hadamard(n):
    return np.where(_popcount_and(n) % 2 == 0, 1.0, -1.0).astype(np.float32)


def _h64_i2():
    h64 = _hadamard(64)
    m = np.zeros((128, 128), dtype=np.float32)
    for r2 in range(2):
        m[r2::2, r2::2] = h64
    return m


def _amask(j):
    m = np.zeros((4, 4, 128, 1024), dtype=np.float32)
    for l in range(4):
        g = S_SETS[j][l]
        kvs = KVS[l]
        for i2 in range(4):
            qpos = g * 512 + i2 * 128 + np.arange(128)[:, None]
            jabs = kvs - 1024 + np.arange(1024)[None, :]
            m[l, i2] = np.where(jabs <= qpos, 0.0, NEG)
    return m.astype(ml_dtypes.bfloat16)


def _ln_tile(R, vS, aS, sX, sXsq, st, sXn, xld_wait, cEps, sq_guard=()):
    R.op("vector", lambda e, sX=sX, st=st: e.reduce_sum(st[:, 0:1], sX, axis=AX.X),
         waits=(xld_wait,), inc=vS)
    R.op("vector", lambda e, st=st: e.tensor_scalar_mul(st[:, 1:2], st[:, 0:1],
                                                        1.0 / D),
         waits=((vS, vS.n),), inc=vS)
    R.op("scalar", lambda e, sX=sX, st=st: e.activation(
        out=sXsq, in_=sX, func=AF.Square, accum_out=st[:, 2:3]),
        waits=(xld_wait,) + tuple(sq_guard), inc=aS)
    asq = (aS, aS.n)
    R.op("vector", lambda e, st=st: e.tensor_mul(st[:, 4:5], st[:, 1:2], st[:, 1:2]),
         waits=((vS, vS.n),), inc=vS)
    R.op("vector", lambda e, st=st: e.scalar_tensor_tensor(
        out=st[:, 3:4], in0=st[:, 2:3], scalar=1.0 / D, in1=st[:, 4:5],
        op0=ALU.mult, op1=ALU.subtract), waits=(asq, (vS, vS.n)), inc=vS)
    vvar = (vS, vS.n)
    R.op("scalar", lambda e, st=st, cEps=cEps: e.activation(
        out=st[:, 5:6], in_=st[:, 3:4], func=AF.Sqrt, bias=cEps[:, 0:1]),
        waits=(vvar,), inc=aS)
    asd = (aS, aS.n)
    R.op("vector", lambda e, st=st: e.reciprocal(st[:, 6:7], st[:, 5:6]),
         waits=(asd,), inc=vS)
    R.op("vector", lambda e, sX=sX, st=st, sXn=sXn: e.tensor_scalar(
        out=sXn, in0=sX, scalar1=st[:, 1:2], scalar2=st[:, 6:7],
        op0=ALU.subtract, op1=ALU.mult), waits=((vS, vS.n),), inc=vS)
    return (vS, vS.n)


def _xn_transposes(R, nc_t, sXn, sXnT, ts, pT, pT_free, tT, vS, cIdB,
                   xn_done, CONST, extra_guard, tkey):
    for k in range(4):
        pb = (tkey * 4 + k) % 2
        for q in range(4):
            dc = 4 * k + q
            w = (xn_done, CONST, pT_free[pb]) + tuple(extra_guard) if q == 0 else ()
            R.op("tensor",
                 lambda e, sXn=sXn, dc=dc, pb=pb, q=q: e.transpose(
                     pT[pb][:, 128 * q:128 * (q + 1)],
                     sXn[:, 128 * dc:128 * (dc + 1)], cIdB[:]),
                 waits=w, inc=tT if q == 3 else None)
        tdone = (tT, tT.n)
        R.op("vector",
             lambda e, k=k, ts=ts, pb=pb: e.tensor_copy(
                 out=sXnT[:, 4 * k:4 * (k + 1), 128 * ts:128 * (ts + 1)],
                 in_=pT[pb][:].rearrange("p (a b) -> p a b", a=4)),
             waits=(tdone,), inc=vS)
        pT_free[pb] = (vS, vS.n)


def build_program(debug=False):
    nc = bass.Bass("TRN2", target_bir_lowering=False, debug=False)

    xf = nc.dram_tensor("xf", [SEQ, D], F32, kind="ExternalInput").ap()
    xo = nc.dram_tensor("xo", [OWN, D], F32, kind="ExternalInput").ap()
    wq = nc.dram_tensor("wq", [D, D], F32, kind="ExternalInput").ap()
    wk = nc.dram_tensor("wk", [D, D], F32, kind="ExternalInput").ap()
    wv = nc.dram_tensor("wv", [D, D], F32, kind="ExternalInput").ap()
    wout = nc.dram_tensor("wout", [D, D], F32, kind="ExternalInput").ap()
    w1 = nc.dram_tensor("w1", [FH, D], F32, kind="ExternalInput").ap()
    w3 = nc.dram_tensor("w3", [FH, D], F32, kind="ExternalInput").ap()
    w2 = nc.dram_tensor("w2", [D, FH], F32, kind="ExternalInput").ap()
    amask = nc.dram_tensor("amask", [4, 4, 128, 1024], BF16, kind="ExternalInput").ap()
    h128 = nc.dram_tensor("h128", [128, 128], F32, kind="ExternalInput").ap()
    h64 = nc.dram_tensor("h64", [128, 128], F32, kind="ExternalInput").ap()
    identf = nc.dram_tensor("identf", [128, 128], F32, kind="ExternalInput").ap()
    out = nc.dram_tensor("out", [OWN, D], F32, kind="ExternalOutput").ap()

    WQB = nc.dram_tensor("WQB", [D, D], BF16, kind="Internal").ap()
    WKB = nc.dram_tensor("WKB", [D, D], BF16, kind="Internal").ap()
    WOB = nc.dram_tensor("WOB", [D, D], BF16, kind="Internal").ap()
    KS = nc.dram_tensor("KS", [D, SEQ], BF16, kind="Internal").ap()
    VS = nc.dram_tensor("VS", [NH, 128, 32, DH], BF16, kind="Internal").ap()
    QS = nc.dram_tensor("QS", [4, 128, NH, 512], BF16, kind="Internal").ap()
    XNT = nc.dram_tensor("XNT", [4, 128, NDC, 512], BF16, kind="Internal").ap()
    XFT = nc.dram_tensor("XFT", [8, 128, NDC, 512], BF16, kind="Internal").ap()
    AS = nc.dram_tensor("AS", [NH, 128, OWN], BF16, kind="Internal").ap()
    W13S = nc.dram_tensor("W13S", [NU, 128, 2 * D], BF16, kind="Internal").ap()
    W2S = nc.dram_tensor("W2S", [NU, 128, D], BF16, kind="Internal").ap()

    dbg = {}
    if debug:
        for name, t, shape in (("dKS", KS, [D, SEQ]), ("dVS", VS, [NH, 128, 32, DH]),
                               ("dQS", QS, [4, 128, NH, 512]),
                               ("dXNT", XNT, [4, 128, NDC, 512]),
                               ("dAS", AS, [NH, 128, OWN]),
                               ("dW13S", W13S, [NU, 128, 2 * D]),
                               ("dW2S", W2S, [NU, 128, D])):
            dbg[name] = (nc.dram_tensor(name, shape, BF16, kind="ExternalOutput").ap(), t)

    with ExitStack() as es_all:
        cIdB = es_all.enter_context(nc.sbuf_tensor("cIdB", [128, 128], BF16)).ap()
        cIdF = es_all.enter_context(nc.sbuf_tensor("cIdF", [128, 128], F32)).ap()
        cH128 = es_all.enter_context(nc.sbuf_tensor("cH128", [128, 128], BF16)).ap()
        cH64 = es_all.enter_context(nc.sbuf_tensor("cH64", [128, 128], BF16)).ap()
        cOnesF = es_all.enter_context(nc.sbuf_tensor("cOnesF", [1, 128], F32)).ap()
        cEps = es_all.enter_context(nc.sbuf_tensor("cEps", [128, 1], F32)).ap()

        blk = es_all.enter_context(nc.Block())
        R = Rec(nc, blk)

        cst = R.sem("cst")
        cstf = R.sem("cstf")
        R.dma("gpsimd", cIdB[:], identf[:], cst)
        R.dma("sync", cIdF[:], identf[:], cstf)
        R.dma("gpsimd", cH128[:], h128[:], cst)
        R.dma("gpsimd", cH64[:], h64[:], cst)
        CONST = (cst, cst.n)
        vcst = R.sem("vcst")
        R.op("vector", lambda e: e.memset(cOnesF[:], 1.0), inc=vcst)
        R.op("vector", lambda e: e.memset(cEps[:], 1.0e-5), inc=vcst)
        ONES = (vcst, vcst.n)

        WKR = R.dma("gpsimd", WKB[:, :], wk[:, :], R.sem("cwk"))
        WQR = R.dma("gpsimd", WQB[:, :], wq[:, :], R.sem("cwq"))
        WOR = R.dma("gpsimd", WOB[:, :], wout[:, :], R.sem("cwo"))

        H = dict(xf=xf, xo=xo, wq=WQB, wk=WKB, wv=wv, wout=WOB, w1=w1,
                 w3=w3, w2=w2, amask=amask, out=out, KS=KS, VS=VS, QS=QS,
                 XNT=XNT, XFT=XFT, AS=AS, W13S=W13S, W2S=W2S, cIdB=cIdB, cIdF=cIdF,
                 cH128=cH128, cH64=cH64, cOnesF=cOnesF, cEps=cEps, CONST=CONST,
                 ONES=ONES, WKR=WKR, WQR=WQR, WOR=WOR)

        R.fence([CONST, ONES, (cstf, cstf.n)])
        f1 = _phase12(nc, R, H)
        R.fence(f1)
        f3 = _phase3(nc, R, H)
        R.fence(f3)
        oSg = _phase4(nc, R, H)

        if debug:
            dsem = R.sem("dbgs")
            for name, (o, t) in dbg.items():
                R.dma("sync", o[:], t[:], dsem)
            R.op("sync", None, waits=((dsem, dsem.n),))

        R.op("sync", None, waits=tuple((s, s.n) for s in oSg))
        R.flush()

    return nc


def _phase12(nc, R, H):
    with ExitStack() as es:
        sX = es.enter_context(nc.sbuf_tensor("sX", [128, 2, D], F32)).ap()
        sStat = es.enter_context(nc.sbuf_tensor("sStat", [128, 2, 8], F32)).ap()
        sXn = es.enter_context(nc.sbuf_tensor("sXn", [128, 2, D], BF16)).ap()
        sXT = es.enter_context(nc.sbuf_tensor("sXT", [128, 2, NDC, 512], BF16)).ap()
        sW = es.enter_context(nc.sbuf_tensor("sW", [128, NDC, D], BF16)).ap()
        sVst = es.enter_context(nc.sbuf_tensor("sVst", [128, NH, 4, DH], BF16)).ap()
        sKev = es.enter_context(nc.sbuf_tensor("sKev", [128, 4, 512], BF16)).ap()
        sWld = es.enter_context(nc.sbuf_tensor("sWld", [128, 2, D], F32)).ap()
        sWldB = es.enter_context(nc.sbuf_tensor("sWldB", [128, 2, D], BF16)).ap()
        sWst = es.enter_context(nc.sbuf_tensor("sWst", [128, 2, D], BF16)).ap()
        sW2ldB = es.enter_context(nc.sbuf_tensor("sW2ldB", [128, 1, D], BF16)).ap()
        sStgA = es.enter_context(nc.sbuf_tensor("sStgA", [128, 16, 512], BF16)).ap()
        sW2ev = es.enter_context(nc.sbuf_tensor("sW2ev", [128, 4, 512], BF16)).ap()
        pT = [es.enter_context(nc.psum_tensor(f"pT{i}", [128, 512], BF16)).ap()
              for i in range(2)]
        pMM = [es.enter_context(nc.psum_tensor(f"pMM{i}", [128, 512], F32)).ap()
               for i in range(2)]
        pWT = [es.enter_context(nc.psum_tensor(f"pWT{i}", [128, 512], BF16)).ap()
               for i in range(2)]
        pSA = [es.enter_context(nc.psum_tensor(f"pSA{i}", [128, 512], F32)).ap()
               for i in range(2)]

        xldg = [R.sem("p1_xld0"), R.sem("p1_xld1")]
        xftg = [R.sem("p1_xft0"), R.sem("p1_xft1")]
        stg = [R.sem("p1_st0"), R.sem("p1_st1")]
        wres = R.sem("p1_wres")
        wldg = [R.sem("p1_wld0"), R.sem("p1_wld1")]
        vS = R.sem("p1_v")
        aS = R.sem("p1_a")
        tT = R.sem("p1_tT")
        tV = R.sem("p1_tV")
        tK = R.sem("p1_tK")
        tW = R.sem("p1_tW")
        tA = R.sem("p1_tA")
        kSg = [R.sem(f"p1_kS{i}") for i in range(4)]
        vsSg = [R.sem(f"p1_vs{i}") for i in range(4)]
        qSg = [R.sem(f"p1_qS{i}") for i in range(4)]
        wSg = [R.sem("p1_wS0"), R.sem("p1_wS1")]
        w2Sg = [R.sem(f"p1_w2S{i}") for i in range(4)]

        xf, xo, w1, w3, w2 = H["xf"], H["xo"], H["w1"], H["w3"], H["w2"]
        wq, wk, wv = H["wq"], H["wk"], H["wv"]
        KS, VS, QS, XNT, XFT = H["KS"], H["VS"], H["QS"], H["XNT"], H["XFT"]
        W13S, W2S = H["W13S"], H["W2S"]
        cIdB, cH128, cEps, CONST = H["cIdB"], H["cH128"], H["cEps"], H["CONST"]

        sX_free = [None, None]
        sXn_guard = [None, None]
        sXT_guard = [None, None]
        pT_free = [None, None]
        pMM_free = [None, None]
        pWT_free = [None, None]
        pSA_free = [None, None]
        sKev_free = [None] * 4
        sWld_free = [None, None]
        sWldB_free = [None, None]
        sWst_free = [None, None]
        sW2ldB_free = [None]
        sW2ev_free = [None] * 4
        sStgA_guard = [None]
        tmark = [[None] * 4, [None] * 4]
        xnt_mark = [None, None]
        state = {"prep": 0}
        pending = []

        def prep_head():
            pi = state["prep"]
            state["prep"] += 1
            if pi >= 128:
                return None
            mat_src = w1 if pi < 64 else w3
            off = 0 if pi < 64 else D
            u = pi % 64
            s = pi % 2
            R.dma("scalar", sWld[:, s, :],
                  mat_src.rearrange("(a rr) d -> a rr d", rr=128)[:, 2 * u:2 * u + 2, :],
                  wldg[s], waits=(sWld_free[s],))
            ld = (wldg[s], wldg[s].n)
            R.op("scalar",
                 lambda e, s=s: e.activation(out=sWldB[:, s, :], in_=sWld[:, s, :],
                                             func=AF.Identity),
                 waits=(ld, sWldB_free[s]), inc=aS)
            cast_done = (aS, aS.n)
            sWld_free[s] = cast_done
            return (pi, u, off, s, cast_done)

        def prep_tail(item):
            pi, u, off, s, cast_done = item
            for k in range(4):
                pb = (pi * 4 + k) % 2
                for q in range(4):
                    dc = 4 * k + q
                    R.op("tensor",
                         lambda e, s=s, dc=dc, pb=pb, q=q: e.transpose(
                             pWT[pb][:, 128 * q:128 * (q + 1)],
                             sWldB[:, s, 128 * dc:128 * (dc + 1)], cIdB[:]),
                         waits=(cast_done, CONST, pWT_free[pb]) if q == 0 else (),
                         inc=tW if q == 3 else None)
                tdone = (tW, tW.n)
                R.op("vector",
                     lambda e, k=k, s=s, pb=pb: e.tensor_copy(
                         out=sWst[:, s, 512 * k:512 * (k + 1)], in_=pWT[pb][:]),
                     waits=(tdone, sWst_free[s]), inc=vS)
                pWT_free[pb] = (vS, vS.n)
            evdone = (vS, vS.n)
            sWldB_free[s] = (tW, tW.n)
            for i in range(2):
                R.dma("sync", W13S[u, 64 * i:64 * (i + 1), off:off + D],
                      sWst[64 * i:64 * (i + 1), s, :], wSg[s],
                      waits=(evdone,) if i == 0 else ())
            sWst_free[s] = wSg[s]

        def drive_preps(nh):
            for item in pending:
                prep_tail(item)
            pending.clear()
            for _ in range(nh):
                item = prep_head()
                if item is not None:
                    pending.append(item)

        w2cd = {}

        def w2_load(wi, j):
            hh, dcb = wi // 4, wi % 4
            dc = 4 * dcb + j
            s = j % 2
            for i in range(4):
                R.dma("scalar", sWld[32 * i:32 * (i + 1), s, :],
                      w2[128 * dc + 32 * i:128 * dc + 32 * (i + 1),
                         2048 * hh:2048 * (hh + 1)],
                      wldg[s], waits=(sWld_free[s],) if i == 0 else ())
            w2cd[(wi, j, "ld")] = (wldg[s], wldg[s].n)

        def w2_castT(wi, j):
            s = j % 2
            R.op("scalar",
                 lambda e, s=s: e.activation(out=sW2ldB[:, 0, :],
                                             in_=sWld[:, s, :],
                                             func=AF.Identity),
                 waits=(w2cd[(wi, j, "ld")], sW2ldB_free[0]), inc=aS)
            cast_done = (aS, aS.n)
            sWld_free[s] = cast_done
            for k in range(4):
                pb = (wi * 16 + j * 4 + k) % 2
                for q in range(4):
                    a = 4 * k + q
                    w = ((cast_done, CONST, pWT_free[pb], sStgA_guard[0])
                         if q == 0 else ())
                    R.op("tensor",
                         lambda e, a=a, pb=pb, q=q: e.transpose(
                             pWT[pb][:, 128 * q:128 * (q + 1)],
                             sW2ldB[:, 0, 128 * a:128 * (a + 1)], cIdB[:]),
                         waits=w, inc=tW if q == 3 else None)
                tdone = (tW, tW.n)
                R.op("vector",
                     lambda e, k=k, j=j, pb=pb: e.tensor_copy(
                         out=sStgA[:, 4 * k:4 * (k + 1), 128 * j:128 * (j + 1)],
                         in_=pWT[pb][:].rearrange("p (a b) -> p a b", a=4)),
                     waits=(tdone,), inc=vS)
                pWT_free[pb] = (vS, vS.n)
            sW2ldB_free[0] = (tW, tW.n)
            w2cd[(wi, "stg", j)] = (vS, vS.n)

        def w2_stageA(wi):
            hh, dcb = wi // 4, wi % 4
            stg_ready = w2cd[(wi, "stg", 3)]
            for k in range(16):
                a = 16 * hh + k
                pb = k % 2
                R.op("tensor",
                     lambda e, k=k, pb=pb: e.matmul(
                         pSA[pb][:], cH128[:], sStgA[:, k, :],
                         start=True, stop=True),
                     waits=(stg_ready, CONST, pSA_free[pb]), inc=tA)
                mmdone = (tA, tA.n)
                eb = k % 4
                R.op("scalar",
                     lambda e, eb=eb, pb=pb: e.activation(
                         out=sW2ev[:, eb, :], in_=pSA[pb][:], func=AF.Identity,
                         scale=SCALE_Q),
                     waits=(mmdone, sW2ev_free[eb]), inc=aS)
                pSA_free[pb] = (aS, aS.n)
                ev = (aS, aS.n)
                for i in range(2):
                    R.dma("scalar",
                          W2S[32 * i:32 * (i + 1), 2 * a:2 * a + 2,
                              512 * dcb:512 * (dcb + 1)],
                          sW2ev[64 * i:64 * (i + 1), eb, :], w2Sg[eb],
                          waits=(ev,) if i == 0 else ())
                sW2ev_free[eb] = w2Sg[eb]
            sStgA_guard[0] = (tA, tA.n)

        w2_steps = []
        for wi in range(16):
            w2_steps.append(lambda wi=wi: w2_load(wi, 0))
            for j in range(4):
                if j < 3:
                    w2_steps.append(lambda wi=wi, j=j: (w2_load(wi, j + 1),
                                                        w2_castT(wi, j)))
                else:
                    w2_steps.append(lambda wi=wi: w2_castT(wi, 3))
            w2_steps.append(lambda wi=wi: w2_stageA(wi))
        w2_pos = [0]

        def drive_w2(n):
            for _ in range(n):
                if w2_pos[0] < len(w2_steps):
                    w2_steps[w2_pos[0]]()
                    w2_pos[0] += 1

        def x_load(src_x, g, ts, tbase):
            t = tbase + 4 * g + ts
            b = t % 2
            for i in range(4):
                R.dma("sync", sX[32 * i:32 * (i + 1), b, :],
                      src_x[128 * (4 * g + ts) + 32 * i:
                            128 * (4 * g + ts) + 32 * (i + 1), :],
                      xldg[b], waits=(sX_free[b],) if i == 0 else ())
            return (b, (xldg[b], xldg[b].n), t)

        def chain_of(tile):
            b, xw, t = tile
            xn_done = _ln_tile(R, vS, aS, sX[:, b, :], sXn[:, b, :],
                               sStat[:, b, :], sXn[:, b, :], xw, cEps,
                               sq_guard=(sXn_guard[b],))
            sX_free[b] = xn_done
            return (b, xn_done, t)

        def tp_part(part, gb):
            b, xn_done, t = part
            _xn_transposes(R, nc, sXn[:, b, :], sXT[:, gb, :, :], t % 4, pT,
                           pT_free, tT, vS, cIdB, xn_done, CONST,
                           (sXT_guard[gb],), t)
            sXn_guard[b] = (tT, tT.n)

        def wv_chunk(c):
            s = c % 2
            wvr = wv.rearrange("(c p) n -> p c n", p=128)
            for i in range(4):
                R.dma("scalar", sWld[32 * i:32 * (i + 1), s, :],
                      wvr[32 * i:32 * (i + 1), c, :],
                      wldg[s], waits=(sWld_free[s],) if i == 0 else ())
            ld = (wldg[s], wldg[s].n)
            R.op("scalar",
                 lambda e, c=c, s=s: e.activation(out=sW[:, c, :],
                                                  in_=sWld[:, s, :],
                                                  func=AF.Identity),
                 waits=(ld,), inc=aS)
            sWld_free[s] = (aS, aS.n)

        def load_resident(mat, ready_sem, pe_done):
            for c16 in range(16):
                R.dma("scalar", sW[:, c16, :],
                      mat.rearrange("(c p) n -> p c n", p=128)[:, c16, :],
                      wres, waits=(pe_done, ready_sem))
            return (wres, wres.n)

        a_tiles = [(0, ts) for ts in range(4)] + \
                  [(g1, vc) for g1 in range(1, 8) for vc in range(4)]
        a_pos = [0]
        pend_tile = [x_load(xf, 0, 0, 0)]

        def next_tile(src_x, tiles, pos, tbase):
            tile = pend_tile[0]
            pos[0] += 1
            if pos[0] < len(tiles):
                g1, ts1 = tiles[pos[0]]
                pend_tile[0] = x_load(src_x, g1, ts1, tbase)
            else:
                pend_tile[0] = None
            return chain_of(tile)

        for ts in range(4):
            part = next_tile(xf, a_tiles, a_pos, 0)
            tp_part(part, 0)
            tmark[0][ts] = (vS, vS.n)
            for c in range(4 * ts, 4 * ts + 4):
                wv_chunk(c)
        wv_rdy = (aS, aS.n)
        A_pe_g6 = None
        for g in range(8):
            gb = g % 2
            R.dma("sync", XFT[g, :, :, :], sXT[:, gb, :, :], stg[gb],
                  waits=(tmark[gb][3],))
            for vc in range(4):
                part = next_tile(xf, a_tiles, a_pos, 0) if g < 7 else None
                for ts4 in range(4):
                    pb = (vc * 4 + ts4) % 2
                    for dc in range(NDC):
                        R.op("tensor",
                             lambda e, gb=gb, vc=vc, ts4=ts4, dc=dc, pb=pb:
                             e.matmul(
                                 pMM[pb][:],
                                 sXT[:, gb, dc, 128 * ts4:128 * (ts4 + 1)],
                                 sW[:, dc, 512 * vc:512 * (vc + 1)],
                                 start=(dc == 0), stop=(dc == NDC - 1)),
                             waits=(tmark[gb][ts4], wv_rdy, pMM_free[pb])
                             if dc == 0 else (),
                             inc=tV if dc == NDC - 1 else None)
                    mmdone = (tV, tV.n)
                    R.op("vector",
                         lambda e, vc=vc, ts4=ts4, pb=pb: e.tensor_copy(
                             out=sVst[:, 4 * vc:4 * (vc + 1), ts4, :],
                             in_=pMM[pb][:].rearrange("p (h r) -> p h r", h=4)),
                         waits=(mmdone, vsSg[vc]), inc=vS)
                    pMM_free[pb] = (vS, vS.n)
                vgrp_done = (vS, vS.n)
                if part is not None:
                    tp_part(part, (g + 1) % 2)
                    tmark[(g + 1) % 2][vc] = (vS, vS.n)
                drive_preps(2)
                for hq in range(4):
                    h = 4 * vc + hq
                    R.dma("scalar", VS[h, :, 4 * g:4 * (g + 1), :],
                          sVst[:, h, :, :], vsSg[vc], waits=(vgrp_done,))
            sXT_guard[gb] = stg[gb]
            if g == 6:
                A_pe_g6 = (tV, tV.n)

        A_pe = (tV, tV.n)
        xft_wait = [None, None]
        xft_wait[0] = R.dma("sync", sXT[:, 0, :, :], XFT[0, :, :, :], xftg[0],
                            waits=(A_pe_g6, stg[0]))
        wk_rdy = load_resident(wk, H["WKR"], A_pe)
        sXT_B_free = [A_pe, A_pe]
        for g in range(8):
            b = g % 2
            if g + 1 < 8:
                nb = (g + 1) % 2
                xft_wait[nb] = R.dma("sync", sXT[:, nb, :, :],
                                     XFT[g + 1, :, :, :], xftg[nb],
                                     waits=(A_pe, stg[nb], sXT_B_free[nb]))
            for kc in range(NDC):
                pb = kc % 2
                for dc in range(NDC):
                    R.op("tensor",
                         lambda e, b=b, kc=kc, dc=dc, pb=pb: e.matmul(
                             pMM[pb][:], sW[:, dc, 128 * kc:128 * (kc + 1)],
                             sXT[:, b, dc, :],
                             start=(dc == 0), stop=(dc == NDC - 1)),
                         waits=(xft_wait[b], wk_rdy, pMM_free[pb])
                         if dc == 0 else (),
                         inc=tK if dc == NDC - 1 else None)
                mmdone = (tK, tK.n)
                eb = kc % 4
                R.op("scalar",
                     lambda e, eb=eb, pb=pb: e.activation(
                         out=sKev[:, eb, :], in_=pMM[pb][:], func=AF.Identity),
                     waits=(mmdone, sKev_free[eb]), inc=aS)
                pMM_free[pb] = (aS, aS.n)
                ev = (aS, aS.n)
                for i in range(2):
                    R.dma("sync",
                          KS[128 * kc + 64 * i:128 * kc + 64 * (i + 1),
                             512 * g:512 * (g + 1)],
                          sKev[64 * i:64 * (i + 1), eb, :], kSg[eb],
                          waits=(ev,) if i == 0 else ())
                sKev_free[eb] = kSg[eb]
                if kc % 2 == 1:
                    drive_preps(1)
            sXT_B_free[b] = (tK, tK.n)
        drive_preps(0)

        B_pe = (tK, tK.n)
        sXT_guard[0] = sXT_guard[1] = None
        c_tiles = [(0, ts) for ts in range(4)] + \
                  [(g1, ts) for g1 in range(1, 4) for ts in range(4)]
        c_pos = [0]
        pend_tile[0] = x_load(xo, 0, 0, 32)
        for ts in range(4):
            part = next_tile(xo, c_tiles, c_pos, 32)
            tp_part(part, 0)
        xnt_mark[0] = (vS, vS.n)
        wq_rdy = load_resident(wq, H["WQR"], B_pe)
        for g in range(4):
            gb = g % 2
            R.dma("sync", XNT[g, :, :, :], sXT[:, gb, :, :], stg[gb],
                  waits=(xnt_mark[gb],))
            for qc in range(NDC):
                part = (next_tile(xo, c_tiles, c_pos, 32)
                        if (1 <= qc <= 4 and g < 3) else None)
                pb = qc % 2
                for dc in range(NDC):
                    R.op("tensor",
                         lambda e, gb=gb, qc=qc, dc=dc, pb=pb: e.matmul(
                             pMM[pb][:], sW[:, dc, 128 * qc:128 * (qc + 1)],
                             sXT[:, gb, dc, :],
                             start=(dc == 0), stop=(dc == NDC - 1)),
                         waits=(xnt_mark[gb], wq_rdy, pMM_free[pb])
                         if dc == 0 else (),
                         inc=tK if dc == NDC - 1 else None)
                mmdone = (tK, tK.n)
                eb = qc % 4
                R.op("scalar",
                     lambda e, eb=eb, pb=pb: e.activation(
                         out=sKev[:, eb, :], in_=pMM[pb][:], func=AF.Identity,
                         scale=SCALE_Q),
                     waits=(mmdone, sKev_free[eb]), inc=aS)
                pMM_free[pb] = (aS, aS.n)
                ev = (aS, aS.n)
                for i in range(2):
                    R.dma("sync", QS[g, 64 * i:64 * (i + 1), qc, :],
                          sKev[64 * i:64 * (i + 1), eb, :], qSg[eb],
                          waits=(ev,) if i == 0 else ())
                sKev_free[eb] = qSg[eb]
                if part is not None:
                    tp_part(part, (g + 1) % 2)
                    if qc == 4:
                        xnt_mark[(g + 1) % 2] = (vS, vS.n)
                drive_w2(2 if qc % 2 == 0 else 1)
            sXT_guard[gb] = stg[gb]

        while state["prep"] < 128 or pending:
            drive_preps(1)
        drive_w2(len(w2_steps))

        R.flush()
        sems = (xldg + xftg + stg + kSg + qSg + w2Sg + wldg + wSg + vsSg +
                [wres, vS, aS, tT, tV, tK, tW, tA])
        return [(s, s.n) for s in sems]


def _phase3(nc, R, H):
    with ExitStack() as es:
        sAmL = es.enter_context(nc.sbuf_tensor("sAmL", [128, 4, 4, 1024], BF16)).ap()
        sS = es.enter_context(nc.sbuf_tensor("sS", [128, 4, 4096], F32)).ap()
        sP = es.enter_context(nc.sbuf_tensor("sP", [128, 2, 4, 4096], BF16)).ap()
        sPT = es.enter_context(nc.sbuf_tensor("sPT", [128, 4, 512], BF16)).ap()
        sK = es.enter_context(nc.sbuf_tensor("sK", [128, 2, 4096], BF16)).ap()
        sV = es.enter_context(nc.sbuf_tensor("sV", [128, 2, 32, DH], BF16)).ap()
        sQc = es.enter_context(nc.sbuf_tensor("sQc", [128, 2, 512], BF16)).ap()
        sCM = es.enter_context(nc.sbuf_tensor("sCM", [128, 4, 4], F32)).ap()
        sNM = es.enter_context(nc.sbuf_tensor("sNM", [128, 2, 4], F32)).ap()
        sL = es.enter_context(nc.sbuf_tensor("sL", [128, 2, 4], F32)).ap()
        sRi = es.enter_context(nc.sbuf_tensor("sRi", [128, 2, 4], F32)).ap()
        sA = es.enter_context(nc.sbuf_tensor("sA", [128, 2, 512], BF16)).ap()
        pS = [es.enter_context(nc.psum_tensor(f"p3pS{i}", [128, 2, 512], F32)).ap()
              for i in range(2)]
        pTT = es.enter_context(nc.psum_tensor("p3pTT", [128, 2, 512], BF16)).ap()
        pO = [es.enter_context(nc.psum_tensor(f"p3pO{i}", [128, 512], F32)).ap()
              for i in range(2)]

        dLg = [R.sem("p3_dL0"), R.sem("p3_dL1")]
        mld = R.sem("p3_mld")
        dAg = [R.sem("p3_dA0"), R.sem("p3_dA1")]
        tQK = R.sem("p3_tQK")
        tTP = R.sem("p3_tTP")
        tPV = R.sem("p3_tPV")
        aSE = R.sem("p3_aSE")
        aPT = R.sem("p3_aPT")
        aE = R.sem("p3_aE")
        aA = R.sem("p3_aA")
        vSE = R.sem("p3_vSE")
        vSum = R.sem("p3_vSum")
        gP = R.sem("p3_gP")
        gM = R.sem("p3_gM")

        KS, VS, QS, AS, amask = H["KS"], H["VS"], H["QS"], H["AS"], H["amask"]
        CONST = H["CONST"]

        R.dma("sync", sAmL[:, :, :, :],
              amask.rearrange("l i p w -> p l i w"), mld)
        MASKS = (mld, mld.n)
        ld_free = [None, None]
        pS_free = [None, None]
        pTT_free = [None]
        pO_free = [None, None]
        sA_free = [None]
        sPT_base = [0]
        sNM_guard = [None, None]
        sL_guard = [None, None]
        sRi_guard = [None, None]
        state = {}
        units = [(l, h) for h in range(NH) for l in range(4)]

        def deferred(p):
            l, h, kv, nb, n5, par, _exp_done, scale_done, _tTf = state[p]
            ev_jb = {}
            pv_base = [None]

            def emit_Tpair(jp):
                for sub in range(2):
                    jb = 2 * jp + sub
                    for i2 in range(4):
                        R.op("tensor",
                             lambda e, jb=jb, i2=i2, sub=sub, par=par: e.transpose(
                                 pTT[:, sub, 128 * i2:128 * (i2 + 1)],
                                 sP[:, par, i2, 128 * jb:128 * (jb + 1)],
                                 H["cIdB"][:]),
                             waits=(scale_done, CONST, pTT_free[0])
                             if (sub == 0 and i2 == 0) else (),
                             inc=tTP if (sub == 1 and i2 == 3) else None)
                tdone = (tTP, tTP.n)
                slot = (2 * jp) % 4
                w = [tdone]
                if pv_base[0] is None:
                    pv_base[0] = tPV.n
                if jp >= 2:
                    w.append((tPV, pv_base[0] + 2 * jp - 2))
                if jp % 2 == 0:
                    R.op("vector",
                         lambda e, jp=jp, slot=slot: e.tensor_copy(
                             out=sPT[:, slot:slot + 2, :], in_=pTT[:, :, :]),
                         waits=tuple(w), inc=vSE)
                    ev_jb[jp] = (vSE, vSE.n)
                else:
                    R.op("scalar",
                         lambda e, jp=jp, slot=slot: e.activation(
                             out=sPT[:, slot:slot + 2, :], in_=pTT[:, :, :],
                             func=AF.Identity),
                         waits=tuple(w), inc=aPT)
                    ev_jb[jp] = (aPT, aPT.n)
                pTT_free[0] = ev_jb[jp]

            def emit_PV(jb):
                w = [ev_jb[jb // 2]]
                if jb == 0:
                    w.append(pO_free[par])
                R.op("tensor",
                     lambda e, par=par, jb=jb, nb=nb: e.matmul(
                         pO[par][:], sV[:, par, jb, :], sPT[:, jb % 4, :],
                         start=(jb == 0), stop=(jb == nb - 1)),
                     waits=tuple(w), inc=tPV)

            steps = [lambda: emit_Tpair(0)]
            for jp in range(1, nb // 2):
                steps.append(lambda jp=jp: (emit_Tpair(jp), emit_PV(2 * jp - 2),
                                            emit_PV(2 * jp - 1)))
            steps.append(lambda: (emit_PV(nb - 2), emit_PV(nb - 1)))

            def tail():
                pv_done = (tPV, tPV.n)
                ld_free[par] = pv_done
                R.op("scalar",
                     lambda e, par=par: e.activation(
                         out=sA[:, 0, :], in_=pO[par][:], func=AF.Identity),
                     waits=(pv_done, sA_free[0]), inc=aA)
                pO_free[par] = (aA, aA.n)
                R.dma("scalar", AS[h, :, 512 * l:512 * (l + 1)], sA[:, 0, :],
                      dAg[par], waits=((aA, aA.n),))
                sA_free[0] = dAg[par]
            return steps, tail

        loads_mark = {}

        def emit_loads(j):
            if j >= len(units):
                return
            lj, hj = units[j]
            kvj = KVS[lj]
            nbj = kvj // 128
            pj = j % 2
            R.dma("sync", sK[:, pj, 0:kvj],
                  KS[128 * hj:128 * (hj + 1), 0:kvj],
                  dLg[pj], waits=(ld_free[pj],))
            R.dma("sync", sV[:, pj, 0:nbj, :], VS[hj, :, 0:nbj, :], dLg[pj])
            R.dma("sync", sQc[:, pj, :], QS[lj, :, hj, :], dLg[pj])
            loads_mark[j] = (dLg[pj], dLg[pj].n)

        emit_loads(0)
        for idx, (l, h) in enumerate(units):
            kv = KVS[l]
            nb = kv // 128
            n5 = kv // 512
            par = idx % 2
            mask_ready = MASKS
            lds = loads_mark[idx]

            prev_exp_rows = state[idx - 1][6] if idx > 0 else [None] * 4
            pexp_guard = state[idx - 1][8] if idx > 1 else None
            row_started = [False] * 4
            exp_rows = [None] * 4
            bcnt = [0]
            nbat = n5 // 2

            def emit_qk_batch(i2, bi, idx=idx, l=l, kv=kv, n5=n5, nbat=nbat,
                              par=par, lds=lds, mask_ready=mask_ready,
                              prev_exp_rows=prev_exp_rows,
                              pexp_guard=pexp_guard):
                pp = bcnt[0] % 2
                for k in range(2):
                    j5 = 2 * bi + k
                    R.op("tensor",
                         lambda e, par=par, i2=i2, j5=j5, k=k, pp=pp: e.matmul(
                             pS[pp][:, k, :],
                             sQc[:, par, 128 * i2:128 * (i2 + 1)],
                             sK[:, par, 512 * j5:512 * (j5 + 1)],
                             start=True, stop=True),
                         waits=(lds, pS_free[pp]) if k == 0 else (),
                         inc=tQK if k == 1 else None)
                mmdone = (tQK, tQK.n)
                j0 = 2 * bi
                w = [mmdone]
                if not row_started[i2]:
                    w.append(prev_exp_rows[i2])
                    row_started[i2] = True
                if bi == nbat - 1:
                    w.append(mask_ready)
                    R.op("vector",
                         lambda e, i2=i2, j0=j0, pp=pp, l=l: e.tensor_tensor(
                             out=sS[:, i2, 512 * j0:512 * (j0 + 2)]
                             .rearrange("p (a b) -> p a b", a=2),
                             in0=pS[pp][:, :, :],
                             in1=sAmL[:, l, i2, :]
                             .rearrange("p (a b) -> p a b", a=2),
                             op=ALU.add),
                         waits=tuple(w), inc=vSE)
                    ev = (vSE, vSE.n)
                elif bcnt[0] % 4 == 3:
                    R.op("vector",
                         lambda e, i2=i2, j0=j0, pp=pp: e.tensor_copy(
                             out=sS[:, i2, 512 * j0:512 * (j0 + 2)],
                             in_=pS[pp][:, :, :]),
                         waits=tuple(w), inc=vSE)
                    ev = (vSE, vSE.n)
                else:
                    R.op("scalar",
                         lambda e, i2=i2, j0=j0, pp=pp: e.activation(
                             out=sS[:, i2, 512 * j0:512 * (j0 + 2)],
                             in_=pS[pp][:, :, :], func=AF.Identity),
                         waits=tuple(w), inc=aSE)
                    ev = (aSE, aSE.n)
                pS_free[pp] = ev
                R.op("vector",
                     lambda e, i2=i2, bi=bi, j0=j0: e.reduce_max(
                         sCM[:, i2, bi:bi + 1],
                         sS[:, i2, 512 * j0:512 * (j0 + 2)], axis=AX.X),
                     waits=(ev,), inc=vSE)
                bcnt[0] += 1
                if bi == nbat - 1:
                    R.op("vector",
                         lambda e, i2=i2, nbat=nbat, par=par: e.reduce_max(
                             sNM[:, par, i2:i2 + 1], sCM[:, i2, 0:nbat],
                             axis=AX.X, negate=True),
                         waits=((vSE, vSE.n), sNM_guard[par]), inc=vSE)
                    nm = (vSE, vSE.n)
                    R.op("scalar",
                         lambda e, i2=i2, kv=kv, par=par: e.activation(
                             out=sP[:, par, i2, 0:kv], in_=sS[:, i2, 0:kv],
                             func=AF.Exp, bias=sNM[:, par, i2:i2 + 1],
                             accum_out=sL[:, par, i2:i2 + 1]),
                         waits=(nm, sL_guard[par], pexp_guard), inc=aE)
                    exp_rows[i2] = (aE, aE.n)
                    R.op("vector",
                         lambda e, i2=i2, par=par: e.reciprocal(
                             sRi[:, par, i2:i2 + 1], sL[:, par, i2:i2 + 1]),
                         waits=((aE, aE.n), sRi_guard[par]), inc=vSum)
                    R.op("vector",
                         lambda e, i2=i2, kv=kv, par=par: e.tensor_scalar_mul(
                             sP[:, par, i2, 0:kv], sP[:, par, i2, 0:kv],
                             sRi[:, par, i2:i2 + 1]),
                         waits=((vSum, vSum.n),), inc=gP)

            qk_steps = [(i2, bi) for i2 in range(4) for bi in range(nbat)]
            if idx > 0:
                pair_steps, def_tail = deferred(idx - 1)
            else:
                pair_steps, def_tail = [], None
            nq, nd = len(qk_steps), len(pair_steps)
            qi = di = 0
            while qi < nq or di < nd:
                take_q = 2 if qi == 0 else 1
                for _ in range(take_q):
                    if qi < nq:
                        emit_qk_batch(*qk_steps[qi])
                        qi += 1
                if nq - qi <= 0:
                    while di < nd:
                        pair_steps[di]()
                        di += 1
                else:
                    want = int(round((qi / nq) * nd))
                    while di < min(want, nd):
                        pair_steps[di]()
                        di += 1
            if def_tail is not None:
                def_tail()
            emit_loads(idx + 1)

            sNM_guard[par] = (aE, aE.n)
            sL_guard[par] = (vSum, vSum.n)
            sRi_guard[par] = (gP, gP.n)
            state[idx] = (l, h, kv, nb, n5, par, exp_rows, (gP, gP.n),
                          (tTP, tTP.n))
        last_steps, last_tail = deferred(len(units) - 1)
        for st_fn in last_steps:
            st_fn()
        last_tail()

        R.flush()
        sems = dLg + dAg + [mld, tQK, tTP, tPV, aSE, aPT, aE, aA, vSE,
                            vSum, gP]
        return [(s, s.n) for s in sems]


def _phase4(nc, R, H):
    with ExitStack() as es:
        sH2 = es.enter_context(nc.sbuf_tensor("sH2", [128, NU, 512], BF16)).ap()
        sXnTc = es.enter_context(nc.sbuf_tensor("sXnTc", [128, NDC, 512], BF16)).ap()
        sW13t = es.enter_context(nc.sbuf_tensor("sW13t", [128, 3, 2 * D], BF16)).ap()
        sW2t = es.enter_context(nc.sbuf_tensor("sW2t", [128, 4, 1024], BF16)).ap()
        sWoT = es.enter_context(nc.sbuf_tensor("sWoT", [128, 2, 1024], BF16)).ap()
        sAt = es.enter_context(nc.sbuf_tensor("sAt", [128, 2, 512], BF16)).ap()
        sSil = es.enter_context(nc.sbuf_tensor("sSil", [128, 2, 512], F32)).ap()
        sHb = es.enter_context(nc.sbuf_tensor("sHb", [128, 2, 512], BF16)).ap()
        sFM = es.enter_context(nc.sbuf_tensor("sFM", [128, NDC, 512], F32)).ap()
        sXo = es.enter_context(nc.sbuf_tensor("p4sXo", [128, 2, D], F32)).ap()
        sOutT = es.enter_context(nc.sbuf_tensor("sOutT", [128, 2, D], F32)).ap()

        xnl = R.sem("p4_xnl")
        wlg = [R.sem("p4_wl0"), R.sem("p4_wl1"), R.sem("p4_wl2")]
        w2tg = [R.sem(f"p4_w2t{i}") for i in range(4)]
        atg = [R.sem("p4_at0"), R.sem("p4_at1")]
        glg = [R.sem("p4_gl0"), R.sem("p4_gl1")]
        xlg = [R.sem("p4_xl0"), R.sem("p4_xl1")]
        oSg = [R.sem("p4_oS0"), R.sem("p4_oS1")]
        tU = R.sem("p4_tU")
        tL = R.sem("p4_tL")
        tBB = R.sem("p4_tB")
        tG = R.sem("p4_tG")
        tOT = R.sem("p4_tOT")
        aSi = R.sem("p4_aSi")
        aH2 = R.sem("p4_aH2")
        aG = R.sem("p4_aG")
        vMu = R.sem("p4_vMu")
        vO = R.sem("p4_vO")

        xo, wout = H["xo"], H["wout"]
        XNT, AS, W13S, W2S = H["XNT"], H["AS"], H["W13S"], H["W2S"]
        cH64, cIdF, out, CONST = H["cH64"], H["cIdF"], H["out"], H["CONST"]

        sW13_free = [None, None, None]
        sW2_free = [None, None, None, None]
        sWo_free = [None, None]
        sAt_free = [None, None]
        sSil_free = [None, None]
        sHb_free = [None, None]
        sXo_free = [None, None]
        sOutT_free = [None, None]
        sH2_guard = [None]
        sXnTc_guard = [None]
        sFM_guard = [None]
        bank_guard = [None]

        for c in range(4):
            R.dma("sync", sXnTc[:, :, :], XNT[c, :, :, :], xnl,
                  waits=(sXnTc_guard[0],))
            xnt_ready = (xnl, xnl.n)

            with ExitStack() as es_b:
                pUp = [es_b.enter_context(
                    nc.psum_tensor(f"p4pUp{c}{i}", [128, 512], F32)).ap()
                    for i in range(2)]
                pLin = [es_b.enter_context(
                    nc.psum_tensor(f"p4pLin{c}{i}", [128, 512], F32)).ap()
                    for i in range(2)]
                pB = [es_b.enter_context(
                    nc.psum_tensor(f"p4pB{c}{i}", [128, 512], F32)).ap()
                    for i in range(2)]
                pUp_free = [bank_guard[0], bank_guard[0]]
                pLin_free = [bank_guard[0], bank_guard[0]]
                pB_free = [bank_guard[0], bank_guard[0]]
                mul_done = {}

                def stage_b(u):
                    b = u % 2
                    R.op("tensor",
                         lambda e, b=b: e.matmul(
                             pB[b][:], cH64[:], sHb[:, b, :],
                             start=True, stop=True),
                         waits=(mul_done[u], CONST, pB_free[b], sH2_guard[0]),
                         inc=tBB)
                    sHb_free[b] = (tBB, tBB.n)
                    R.op("scalar",
                         lambda e, b=b, u=u: e.activation(
                             out=sH2[:, u, :], in_=pB[b][:], func=AF.Copy,
                             scale=0.125),
                         waits=((tBB, tBB.n),), inc=aH2)
                    pB_free[b] = (aH2, aH2.n)

                for u in range(NU):
                    b = u % 2
                    b3 = u % 3
                    R.dma("sync", sW13t[:, b3, :], W13S[u, :, :], wlg[b3],
                          waits=(sW13_free[b3],))
                    wld = (wlg[b3], wlg[b3].n)
                    for dc in range(NDC):
                        R.op("tensor",
                             lambda e, b=b, b3=b3, dc=dc: e.matmul(
                                 pUp[b][:], sW13t[:, b3, 128 * dc:128 * (dc + 1)],
                                 sXnTc[:, dc, :], start=(dc == 0),
                                 stop=(dc == NDC - 1)),
                             waits=(wld, xnt_ready, pUp_free[b]) if dc == 0 else (),
                             inc=tU if dc == NDC - 1 else None)
                    for dc in range(NDC):
                        R.op("tensor",
                             lambda e, b=b, b3=b3, dc=dc: e.matmul(
                                 pLin[b][:], sW13t[:, b3, D + 128 * dc:D + 128 * (dc + 1)],
                                 sXnTc[:, dc, :], start=(dc == 0),
                                 stop=(dc == NDC - 1)),
                             waits=(pLin_free[b],) if dc == 0 else (),
                             inc=tL if dc == NDC - 1 else None)
                    sW13_free[b3] = (tL, tL.n)
                    if u > 0:
                        stage_b(u - 1)
                    R.op("scalar",
                         lambda e, b=b: e.activation(
                             out=sSil[:, b, :], in_=pUp[b][:], func=AF.Silu),
                         waits=((tU, tU.n), sSil_free[b]), inc=aSi)
                    pUp_free[b] = (aSi, aSi.n)
                    R.op("vector",
                         lambda e, b=b: e.tensor_tensor(
                             out=sHb[:, b, :], in0=pLin[b][:],
                             in1=sSil[:, b, :], op=ALU.mult),
                         waits=((tL, tL.n), (aSi, aSi.n), sHb_free[b]), inc=vMu)
                    pLin_free[b] = (vMu, vMu.n)
                    sSil_free[b] = (vMu, vMu.n)
                    mul_done[u] = (vMu, vMu.n)
                stage_b(NU - 1)
                h2_ready = (aH2, aH2.n)
                sXnTc_guard[0] = (tL, tL.n)
                R.flush()

            with ExitStack() as es_c:
                pG = es_c.enter_context(
                    nc.psum_tensor(f"p4pG{c}", [128, 8, 512], F32)).ap()
                pG_free = (aH2, aH2.n)
                for half in range(2):
                    for u in range(NU):
                        b = (half * NU + u) % 4
                        R.dma("scalar", sW2t[:, b, :],
                              W2S[u, :, 1024 * half:1024 * (half + 1)], w2tg[b],
                              waits=(sW2_free[b],))
                        wld = (w2tg[b], w2tg[b].n)
                        for d8 in range(8):
                            w = ()
                            if u == 0:
                                w = (wld, h2_ready, sFM_guard[0], pG_free)
                            elif d8 == 0:
                                w = (wld,)
                            R.op("tensor",
                                 lambda e, b=b, d8=d8, u=u: e.matmul(
                                     pG[:, d8, :],
                                     sW2t[:, b, 128 * d8:128 * (d8 + 1)],
                                     sH2[:, u, :], start=(u == 0), stop=False),
                                 waits=w, inc=tG if d8 == 7 else None)
                        sW2_free[b] = (tG, tG.n)
                    for cc in range(NDC):
                        b = cc % 2
                        R.dma("scalar", sWoT[:, b, :],
                              wout[128 * cc:128 * (cc + 1),
                                   1024 * half:1024 * (half + 1)],
                              glg[b], waits=(sWo_free[b], H["WOR"]))
                        R.dma("scalar", sAt[:, b, :],
                              AS[cc, :, 512 * c:512 * (c + 1)], atg[b],
                              waits=(sAt_free[b],))
                        wld2 = ((glg[b], glg[b].n), (atg[b], atg[b].n))
                        for d8 in range(8):
                            R.op("tensor",
                                 lambda e, b=b, d8=d8, cc=cc: e.matmul(
                                     pG[:, d8, :],
                                     sWoT[:, b, 128 * d8:128 * (d8 + 1)],
                                     sAt[:, b, :], start=False,
                                     stop=(cc == NDC - 1)),
                                 waits=wld2 if d8 == 0 else (),
                                 inc=tG if d8 == 7 else None)
                        sWo_free[b] = (tG, tG.n)
                        sAt_free[b] = (tG, tG.n)
                    grp_done = (tG, tG.n)
                    R.op("scalar",
                         lambda e, half=half: e.activation(
                             out=sFM[:, 8 * half:8 * (half + 1), :],
                             in_=pG[:, :, :], func=AF.Copy),
                         waits=(grp_done,), inc=aG)
                    pG_free = (aG, aG.n)
                sH2_guard[0] = (tG, tG.n)
                fm_ready = (aG, aG.n)
                R.flush()

            with ExitStack() as es_d:
                pOq = es_d.enter_context(
                    nc.psum_tensor(f"p4pOq{c}", [128, 4, 512], F32)).ap()
                pOq_free = [(aG, aG.n)] * 4
                for ts in range(4):
                    b = ts % 2
                    t = 4 * c + ts
                    R.dma("sync", sXo[:, b, :], xo[128 * t:128 * (t + 1), :],
                          xlg[b], waits=(sXo_free[b],))
                    xw = (xlg[b], xlg[b].n)
                    for dc in range(NDC):
                        qb = dc // 4
                        R.op("tensor",
                             lambda e, dc=dc, ts=ts, qb=qb: e.transpose(
                                 pOq[:, qb, 128 * (dc % 4):128 * (dc % 4 + 1)],
                                 sFM[:, dc, 128 * ts:128 * (ts + 1)], cIdF[:]),
                             waits=(fm_ready, CONST, pOq_free[qb])
                             if dc % 4 == 0 else (),
                             inc=tOT if dc % 4 == 3 else None)
                    tdone = (tOT, tOT.n)
                    for qb in range(4):
                        R.op("vector",
                             lambda e, qb=qb, b=b: e.tensor_tensor(
                                 out=sOutT[:, b, 512 * qb:512 * (qb + 1)],
                                 in0=pOq[:, qb, :],
                                 in1=sXo[:, b, 512 * qb:512 * (qb + 1)],
                                 op=ALU.add),
                             waits=(tdone, xw, sOutT_free[b]), inc=vO)
                        pOq_free[qb] = (vO, vO.n)
                    sXo_free[b] = (vO, vO.n)
                    R.dma("sync", out[128 * t:128 * (t + 1), :], sOutT[:, b, :],
                          oSg[b], waits=((vO, vO.n),))
                    sOutT_free[b] = oSg[b]
                sFM_guard[0] = (tOT, tOT.n)
                bank_guard[0] = (vO, vO.n)
                R.flush()

        return oSg


def kernel(L0_attn_x, L0_attn_w_qkv, L0_attn_w_out, L0_ffn_W1, L0_ffn_W2,
           L0_ffn_W3):
    x = np.asarray(L0_attn_x, dtype=np.float32)
    wqkv = np.asarray(L0_attn_w_qkv, dtype=np.float32)
    wq = np.ascontiguousarray(wqkv[:, 0:D])
    wk = np.ascontiguousarray(wqkv[:, D:2 * D])
    wv = np.ascontiguousarray(wqkv[:, 2 * D:3 * D])
    wout = np.ascontiguousarray(np.asarray(L0_attn_w_out, dtype=np.float32))
    w1 = np.ascontiguousarray(np.asarray(L0_ffn_W1, dtype=np.float32))
    w2 = np.ascontiguousarray(np.asarray(L0_ffn_W2, dtype=np.float32))
    w3 = np.ascontiguousarray(np.asarray(L0_ffn_W3, dtype=np.float32))

    h128c = _hadamard(128)
    h64c = _h64_i2()
    identf = np.eye(128, dtype=np.float32)
    masks = [np.ascontiguousarray(_amask(j)) for j in (0, 1)]

    nc = build_program()
    in_maps = []
    for c in range(8):
        b, j = c // 2, c % 2
        xb = np.ascontiguousarray(x[b])
        xov = np.ascontiguousarray(
            x[b].reshape(8, 512, D)[S_SETS[j]].reshape(OWN, D))
        in_maps.append(dict(xf=xb, xo=xov, wq=wq, wk=wk, wv=wv, wout=wout,
                            w1=w1, w3=w3, w2=w2, amask=masks[j], h128=h128c,
                            h64=h64c, identf=identf))
    res = run_bass_kernel_spmd(nc, in_maps, core_ids=list(range(8)))
    outa = np.empty((4, SEQ, D), dtype=np.float32)
    for c in range(8):
        b, j = c // 2, c % 2
        oc = res.results[c]["out"].reshape(4, 512, D)
        for l, g in enumerate(S_SETS[j]):
            outa[b, 512 * g:512 * (g + 1)] = oc[l]
    return outa
